# version 06
# baseline (speedup 1.0000x reference)
import contextlib
import numpy as np
import ml_dtypes
import concourse.bass as bass
import concourse.mybir as mybir
from concourse.bass_utils import run_bass_kernel_spmd

F32 = mybir.dt.float32
BF16 = mybir.dt.bfloat16
AF = mybir.ActivationFunctionType
OP = mybir.AluOpType
BFNP = ml_dtypes.bfloat16

T = 2048
D = 1024
DI = 2048
NSTATE = 16
TS = 256
NSL = T // TS
NG = DI // 8
CTRL = 6688
CTRLP = 6784
H = 16
HD = 64
L = 64
NPAIR = D // 2
MAGIC = 12582912.0

DEBUG_PHASES = 99


def _host_prepare(inputs):
    I = inputs
    f32 = np.float32
    asc = np.ascontiguousarray
    pre = {}
    pre["W_inT"] = asc(I["L0_attn_a_W_in"].astype(f32).T)
    cw = I["L0_attn_a_conv_w"].astype(f32)[:, 0, :]
    pre["conv_w_l"] = asc(cw.reshape(16, 128, 4).transpose(1, 0, 2).reshape(128, 64))
    pre["conv_b_l"] = asc(I["L0_attn_a_conv_b"].astype(f32).reshape(16, 128).T)
    wxt = I["L0_attn_a_W_x"].astype(f32).T
    wxp = np.zeros((DI, 128), f32)
    wxp[:, 0:16] = wxt[:, 64:80]
    wxp[:, 32:48] = wxt[:, 80:96]
    wxp[:, 64:128] = wxt[:, 0:64]
    pre["W_xT"] = asc(wxp)
    pre["W_dtT"] = asc(I["L0_attn_a_W_dt"].astype(f32).T)
    pre["b_dt_l"] = asc(I["L0_attn_a_b_dt"].astype(f32).reshape(16, 128).T)
    pre["A_log_g"] = asc(I["L0_attn_a_A_log"].astype(f32).reshape(NG, 128))
    pre["D_param_l"] = asc(I["L0_attn_a_D_param"].astype(f32).reshape(16, 128).T)
    pre["W_outaT"] = asc(I["L0_attn_a_W_out"].astype(f32).T)
    pre["gate_w"] = asc(I["L0_ffn_gate_w"].astype(f32))
    pre["w0T"] = asc(I["L0_ffn_w0"].astype(f32).T)
    pre["b0_l"] = asc(I["L0_ffn_b0"].astype(f32).reshape(8, 128).T)
    pre["w1T"] = asc(I["L0_ffn_w1"].astype(f32).T)
    pre["b1_l"] = asc(I["L0_ffn_b1"].astype(f32).reshape(8, 128).T)
    SIZES = [D, D // 2, D, D, D, D, H, H, D]
    OFF = np.cumsum([0] + SIZES)
    perm = np.concatenate([
        np.arange(OFF[0], OFF[1])[0::2], np.arange(OFF[0], OFF[1])[1::2],
        np.arange(OFF[1], OFF[2]),
        np.arange(OFF[2], OFF[3])[0::2], np.arange(OFF[2], OFF[3])[1::2],
        np.arange(OFF[3], OFF[4]), np.arange(OFF[4], OFF[5]),
        np.arange(OFF[5], OFF[6]), np.arange(OFF[6], OFF[7]),
        np.arange(OFF[7], OFF[8]), np.arange(OFF[8], OFF[9])])
    wc = I["L0_attn_b_W_ctrl"].astype(f32)[:, perm]
    bc = I["L0_attn_b_b_ctrl"].astype(f32)[perm]
    wcp = np.zeros((D, CTRLP), f32)
    bcp = np.zeros(CTRLP, f32)
    wcp[:, 0:5632] = wc[:, 0:5632]
    bcp[0:5632] = bc[0:5632]
    wcp[:, 5632:5648] = wc[:, 5632:5648]
    bcp[5632:5648] = bc[5632:5648]
    wcp[:, 5696:5712] = wc[:, 5648:5664]
    bcp[5696:5712] = bc[5648:5664]
    wcp[:, 5760:6784] = wc[:, 5664:6688]
    bcp[5760:6784] = bc[5664:6688]
    pre["W_ctrlT_p"] = asc(wcp)
    pre["b_ctrl_cols"] = asc(bcp.reshape(53, 128).T)
    ps128 = np.zeros((2, 128), f32)
    ps128[0, 0:64] = 1; ps128[1, 64:128] = 1
    pre["PAIRSEL128F"] = ps128
    mA = np.where(np.tril(np.ones((L, L)), -1) > 0, 0.0, -1e30).astype(f32)
    mG = np.where(np.triu(np.ones((L, L)), 0) > 0, 0.0, -1e30).astype(f32)
    pre["MBIAS_A"] = asc(np.concatenate([mA, mA], 0))
    pre["MBIAS_G"] = asc(np.concatenate([mG, mG], 0))
    pre["W_block_l"] = asc(I["L0_attn_b_W_block"].astype(f32).reshape(NPAIR, 4)
                           .reshape(4, 128, 4).transpose(1, 0, 2).reshape(128, 16))
    for nm, key in (("qconv", "L0_attn_b_qconv_w"), ("kconv", "L0_attn_b_kconv_w"),
                    ("vconv", "L0_attn_b_vconv_w")):
        c = I[key].astype(f32)
        pre[nm + "_l"] = asc(c.reshape(8, 128, 4).transpose(1, 0, 2).reshape(128, 32))
    pre["rms_m_l"] = asc(np.tile(I["L0_attn_b_rms_m_w"].astype(f32), 16)
                         .reshape(8, 128).T)
    pre["rms_v_perm"] = asc(np.concatenate([I["L0_attn_b_rms_v_w"].astype(f32)[0::2],
                                            I["L0_attn_b_rms_v_w"].astype(f32)[1::2]])
                            .reshape(8, 128).T)
    Wo = I["L0_attn_b_W_out"].astype(f32)
    rperm = np.concatenate([np.arange(0, D), D + np.arange(0, D)[0::2],
                            D + np.arange(0, D)[1::2]])
    pre["W_outb_p"] = asc(Wo[rperm, :])
    pre["b_outb_l"] = asc(I["L0_attn_b_b_out"].astype(f32).reshape(8, 128).T)

    def kb(W):
        KT, MT = W.shape
        A = W.reshape(KT // 1024, 8, 128, MT // 128, 128)
        return asc(A.transpose(3, 0, 2, 1, 4).reshape((MT // 128) * (KT // 1024) * 128,
                                                      1024))
    pre["W_in_s"] = kb(pre["W_inT"])
    pre["W_outa_s"] = kb(pre["W_outaT"])
    pre["w0_s"] = kb(pre["w0T"])
    pre["w1_s"] = kb(pre["w1T"])
    pre["W_ctrl_s"] = kb(pre["W_ctrlT_p"])
    pre["W_outb_s"] = kb(pre["W_outb_p"])
    gw = pre["gate_w"]
    pre["gate_s"] = asc(gw.reshape(8, 128, 8).transpose(1, 0, 2).reshape(128, 64))
    pre["identity"] = np.eye(128, dtype=BFNP)
    pre["identity_f32"] = np.eye(128, dtype=f32)
    s64u = np.zeros((8, 64, 128), BFNP)
    negm = np.zeros((8, 64, 128), BFNP)
    sy64 = np.zeros((8, 128, 64), BFNP)
    for lp in range(8):
        for dd in range(8):
            k = lp * 8 + dd
            s64u[lp, k, dd * NSTATE:(dd + 1) * NSTATE] = 1
            negm[lp, k, dd * NSTATE:(dd + 1) * NSTATE] = -1
            sy64[lp, dd * NSTATE:(dd + 1) * NSTATE, k] = 1
    pre["SEL64U"] = asc(s64u.reshape(8 * 64, 128))
    pre["NEGM64"] = asc(negm.reshape(8 * 64, 128))
    pre["SEL64Y"] = asc(sy64.transpose(1, 0, 2).reshape(128, 8 * 64))
    s16 = np.zeros((16, 128), BFNP)
    for n in range(16):
        s16[n, n::NSTATE] = 1
    pre["SEL16"] = s16
    sq = np.zeros((8 * 128, 16), BFNP)
    sqe = np.zeros((16, 8 * 128), BFNP)
    for j in range(8):
        sq[j * 128 + 0:j * 128 + 64, 2 * j] = 1
        sq[j * 128 + 64:j * 128 + 128, 2 * j + 1] = 1
        sqe[2 * j, j * 128 + 0:j * 128 + 64] = 1
        sqe[2 * j + 1, j * 128 + 64:j * 128 + 128] = 1
    pre["SELQ"] = sq
    pre["SELQE"] = sqe
    psel = np.zeros((2, 256), BFNP)
    psel[0, 0:128] = 1
    psel[1, 128:256] = 1
    pre["PAIRSEL"] = psel
    tl = np.tril(np.ones((L, L), f32), -1)
    ti = np.tril(np.ones((L, L), f32), 0)
    pre["MASK_SL"] = np.concatenate([tl, tl], 0)
    pre["MASK_IL"] = np.concatenate([ti, ti], 0)
    return pre


class P:
    def __init__(self, nc, stack):
        self.nc = nc
        self.st = stack
        self.V = nc.vector
        self.S = nc.scalar
        self.TE = nc.tensor
        self.G = nc.gpsimd
        self.Q = nc.sync
        self.sem = {}
        self.tick = {}
        for name in ("V", "S", "TE", "G", "Qd", "Qw"):
            self.sem[name] = stack.enter_context(nc.semaphore("sem_" + name))
            self.tick[name] = 0

    def done(self, name, ins):
        ins.then_inc(self.sem[name], 16 if name.startswith("Q") else 1)
        self.tick[name] += 16 if name.startswith("Q") else 1
        return self.tick[name]

    def wait(self, eng, name, tick=None):
        t = self.tick[name] if tick is None else tick
        if name.startswith("Q") and eng is not self.Q:
            self.Q.wait_ge(self.sem[name], t)
        eng.wait_ge(self.sem[name], t)

    def vbar(self, ins):
        self.done("V", ins)
        self.V.wait_ge(self.sem["V"], self.tick["V"])

    def gbar(self, ins):
        self.done("G", ins)
        self.G.wait_ge(self.sem["G"], self.tick["G"])
        return self.tick["G"]

    def sbar(self, ins):
        self.done("S", ins)
        self.S.wait_ge(self.sem["S"], self.tick["S"])

    def sb(self, name, shape, dtype):
        return self.st.enter_context(self.nc.sbuf_tensor(name, shape, dtype))

    def arena_init(self, nbytes_b, nbytes_f):
        self._arB = self.sb("arenaB", [128, nbytes_b // 2], BF16)
        self._arF = self.sb("arenaF", [128, nbytes_f // 4], F32)
        self._offB = 0
        self._offF = 0

    def arena_reset(self):
        self._offB = 0
        self._offF = 0

    def aB(self, n):
        v = self._arB[:, self._offB:self._offB + n]
        self._offB += n
        assert self._offB <= self._arB.shape[1], "arenaB overflow"
        return v

    def aF(self, n):
        v = self._arF[:, self._offF:self._offF + n]
        self._offF += n
        assert self._offF <= self._arF.shape[1], "arenaF overflow"
        return v

    def psum(self, name, shape, dtype=F32):
        return self.st.enter_context(self.nc.psum_tensor(name, shape, dtype))


def build_program(n_phases=99, dbg=None):
    nc = bass.Bass("TRN2", target_bir_lowering=False)
    dr = {}

    def din(name, shape, dtype=F32):
        dr[name] = nc.dram_tensor(name, list(shape), dtype, kind="ExternalInput").ap()

    din("x_in", (T, D))
    din("W_inT", (D, 4096)); din("conv_w_l", (128, 64)); din("conv_b_l", (128, 16))
    din("W_xT", (DI, 128)); din("W_dtT", (64, DI)); din("b_dt_l", (128, 16))
    din("A_log_g", (NG, 128)); din("D_param_l", (128, 16)); din("W_outaT", (DI, D))
    din("gate_w", (D, 8)); din("w0T", (D, D)); din("b0_l", (128, 8))
    din("w1T", (D, D)); din("b1_l", (128, 8))
    din("W_ctrlT_p", (D, CTRLP)); din("b_ctrl_cols", (128, 53)); din("W_block_l", (128, 16))
    din("PAIRSEL128F", (2, 128)); din("MBIAS_A", (128, L)); din("MBIAS_G", (128, L))
    din("qconv_l", (128, 32)); din("kconv_l", (128, 32)); din("vconv_l", (128, 32))
    din("rms_m_l", (128, 8)); din("rms_v_perm", (128, 8))
    din("W_outb_p", (2 * D, D)); din("b_outb_l", (128, 8))
    din("W_in_s", (4096, 1024)); din("W_outa_s", (2048, 1024))
    din("w0_s", (1024, 1024)); din("w1_s", (1024, 1024))
    din("W_ctrl_s", (6784, 1024)); din("W_outb_s", (2048, 1024))
    din("gate_s", (128, 64))
    din("identity", (128, 128), BF16); din("identity_f32", (128, 128))
    din("SEL64U", (512, 128), BF16); din("NEGM64", (512, 128), BF16); din("SEL64Y", (128, 512), BF16)
    din("SEL16", (16, 128), BF16); din("SELQ", (8 * 128, 16), BF16)
    din("SELQE", (16, 8 * 128), BF16)
    din("MASK_SL", (128, L)); din("MASK_IL", (128, L)); din("PAIRSEL", (2, 256), BF16)
    out_h = nc.dram_tensor("out", [T, D], F32, kind="ExternalOutput").ap()
    dbg_h = {}
    if dbg:
        for nm, width in dbg.items():
            dbg_h[nm] = nc.dram_tensor("dbg_" + nm, [128, width], F32, kind="ExternalOutput").ap()

    with contextlib.ExitStack() as st:
        p = P(nc, st)
        V, S, TE, G, Q = p.V, p.S, p.TE, p.G, p.Q

        stream = p.sb("stream", [128, 8 * T], BF16)

        def stream_c(c, sl=slice(None)):
            base = c * T
            if sl == slice(None):
                return stream[:, base:base + T]
            return stream[:, base + sl.start:base + sl.stop]

        ident = p.sb("ident", [128, 128], BF16)
        identf = p.sb("identf", [128, 128], F32)
        onescol_b = p.sb("onescol_b", [128, 1], BF16)
        onesrow_b = p.sb("onesrow_b", [1, 128], BF16)
        ones64_b = p.sb("ones64_b", [1, 64], BF16)
        sel64u = p.sb("sel64u", [128, 8 * 128], BF16)
        negm64 = p.sb("negm64", [64, 8 * 128], BF16)
        sel64y = p.sb("sel64y", [128, 8 * 64], BF16)
        sel16 = p.sb("sel16", [16, 128], BF16)
        selq = p.sb("selq", [128, 8 * 16], BF16)
        selq8 = p.sb("selq8", [128, 128], BF16)
        selqe = p.sb("selqe", [16, 8 * 128], BF16)
        pairsel = p.sb("pairsel", [2, 256], BF16)
        mask_sl = p.sb("mask_sl", [128, L], F32)
        mask_il = p.sb("mask_il", [128, L], F32)
        ccol = {}
        for cn, cv in [("one", 1.0), ("eps_ln", 1e-5), ("eps_rms", 1e-6),
                       ("eps_q", 1e-12), ("zero", 0.0), ("tiny", 1e-12)]:
            ccol[cn] = p.sb("c_" + cn, [128, 1], F32)
            V.memset(ccol[cn][:, :], cv)
        V.memset(onescol_b[:, :], 1.0)
        V.memset(onesrow_b[:, :], 1.0)
        V.memset(ones64_b[:, :], 1.0)

        psA = p.psum("psA", [128, 2048], F32)
        psB = p.psum("psB", [128, 512], F32)
        psC = p.psum("psC", [128, 512], F32)
        psD = p.psum("psD", [128, 512], F32)
        psE = p.psum("psE", [128, 1024], BF16)
        p.arena_init(2 * 52032, 4 * 2560)

        for tgt, src in [(ident, dr["identity"]), (identf, dr["identity_f32"]),
                         (sel16, dr["SEL16"]),
                         (selqe, dr["SELQE"]), (pairsel, dr["PAIRSEL"]),
                         (mask_sl, dr["MASK_SL"]),
                         (mask_il, dr["MASK_IL"])]:
            p.done("Qd", Q.dma_start(tgt[:, :], src))
        for v in range(8):
            p.done("Qd", Q.dma_start(sel64u[0:64, v * 128:(v + 1) * 128],
                                     dr["SEL64U"][v * 64:(v + 1) * 64, :]))
            p.done("Qd", Q.dma_start(sel64u[64:128, v * 128:(v + 1) * 128],
                                     dr["SEL64U"][v * 64:(v + 1) * 64, :]))
            p.done("Qd", Q.dma_start(negm64[:, v * 128:(v + 1) * 128],
                                     dr["NEGM64"][v * 64:(v + 1) * 64, :]))
        p.done("Qd", Q.dma_start(sel64y[:, :], dr["SEL64Y"]))
        for j in range(8):
            p.done("Qd", Q.dma_start(selq8[:, j * 16:(j + 1) * 16],
                                     dr["SELQ"][j * 128:(j + 1) * 128, :]))

        x_tm_f = p.sb("x_tm_f", [128, D], F32)
        x_tm_b = p.sb("x_tm_b", [128, D], BF16)
        for i in range(T // 128):
            td = p.done("Qd", Q.dma_start(x_tm_f[:, :], dr["x_in"][i * 128:(i + 1) * 128, :]))
            p.wait(V, "Qd", td)
            vt = p.done("V", V.tensor_copy(x_tm_b[:, :], x_tm_f[:, :]))
            p.wait(TE, "V", vt)
            for c in range(8):
                p.wait(TE, "V")
                tt = p.done("TE", TE.transpose(psE[:, 0:128],
                                               x_tm_b[:, c * 128:(c + 1) * 128], ident[:, :]))
                p.wait(V, "TE", tt)
                vt2 = p.done("V", V.tensor_copy(stream_c(c)[:, i * 128:(i + 1) * 128],
                                                psE[:, 0:128]))
            p.wait(Q, "V", vt2)

        wstage = p.sb("wstage", [128, 1024], F32)
        b0_col = p.sb("b0_col", [128, 8], F32)
        b1_col = p.sb("b1_col", [128, 8], F32)
        p.done("Qd", Q.dma_start(b0_col[:, :], dr["b0_l"]))
        p.done("Qd", Q.dma_start(b1_col[:, :], dr["b1_l"]))
        wtile = [p.sb(f"wtile{i}", [128, 1024], BF16) for i in range(2)]

        wstage2 = p.sb("wstage2", [128, 1024], F32)
        wst2 = [wstage, wstage2]
        sw2_n = [0, 0]
        sw2_td = [0, 0]
        sw2_kw = [1024, 1024]
        sw2_cast = [0, 0]
        sw2_prev_te = [0]

        def sw2_pre(dram_ap, b, kw=1024):
            which = sw2_n[0] % 2
            sw2_n[0] += 1
            p.wait(Q, "S", sw2_cast[which])
            sw2_td[which] = p.done("Qw", Q.dma_start(
                wst2[which][:, 0:kw], dram_ap[b * 128:(b + 1) * 128, 0:kw]))
            sw2_kw[which] = kw

        def sw2_consume():
            which = sw2_n[1] % 2
            sw2_n[1] += 1
            kw = sw2_kw[which]
            cur_te = p.tick["TE"]
            p.wait(S, "Qw", sw2_td[which])
            p.wait(S, "TE", sw2_prev_te[0])
            sw2_prev_te[0] = cur_te
            sw2_cast[which] = p.done("S", S.activation(
                wtile[which][:, 0:kw], wst2[which][:, 0:kw], AF.Copy))
            p.wait(TE, "S", sw2_cast[which])
            p.wait(TE, "V")
            return wtile[which]

        xn = p.sb("xn", [128, 8 * TS], BF16)

        def xn_c(c):
            return xn[:, c * TS:(c + 1) * TS]

        x2b = p.sb("x2b", [128, TS], BF16)
        strow = p.sb("strow", [1, 3 * TS], F32)
        stat_b = p.sb("stat_b", [1, 2 * TS], BF16)

        def ln_slice(s, dst_c):
            sl = slice(s * TS, (s + 1) * TS)
            p.wait(TE, "V")
            for c in range(8):
                tt = p.done("TE", TE.matmul(psA[0:1, 0:TS], onescol_b[:, 0:1],
                                            stream_c(c, sl), start=(c == 0), stop=(c == 7)))
            for c in range(8):
                p.wait(S, "TE")
                stk = p.done("S", S.activation(x2b[:, :], stream_c(c, sl), AF.Square))
                p.wait(TE, "S", stk)
                tt = p.done("TE", TE.matmul(psA[0:1, 512:512 + TS], onescol_b[:, 0:1],
                                            x2b[:, :], start=(c == 0), stop=(c == 7)))
            p.wait(V, "TE", tt)
            p.vbar(V.tensor_scalar(strow[0:1, 0:TS], psA[0:1, 0:TS], 1.0 / D, None, OP.mult))
            p.vbar(V.tensor_scalar(strow[0:1, TS:2 * TS], psA[0:1, 512:512 + TS],
                                   1.0 / D, None, OP.mult))
            p.vbar(V.tensor_tensor(strow[0:1, 2 * TS:3 * TS], strow[0:1, 0:TS],
                                   strow[0:1, 0:TS], OP.mult))
            vt = p.done("V", V.tensor_tensor(strow[0:1, TS:2 * TS], strow[0:1, TS:2 * TS],
                                             strow[0:1, 2 * TS:3 * TS], OP.subtract))
            p.wait(S, "V", vt)
            stk = p.done("S", S.activation(strow[0:1, 2 * TS:3 * TS], strow[0:1, TS:2 * TS],
                                           AF.Sqrt, bias=ccol["eps_ln"][0:1, 0:1]))
            p.wait(V, "S", stk)
            p.vbar(V.reciprocal(strow[0:1, TS:2 * TS], strow[0:1, 2 * TS:3 * TS]))
            p.vbar(V.tensor_tensor(strow[0:1, 2 * TS:3 * TS], strow[0:1, 0:TS],
                                   strow[0:1, TS:2 * TS], OP.mult))
            p.vbar(V.tensor_scalar(strow[0:1, 2 * TS:3 * TS], strow[0:1, 2 * TS:3 * TS],
                                   -1.0, None, OP.mult))
            V.tensor_copy(stat_b[0:1, 0:TS], strow[0:1, TS:2 * TS])
            vt = p.done("V", V.tensor_copy(stat_b[0:1, TS:2 * TS], strow[0:1, 2 * TS:3 * TS]))
            p.wait(TE, "V", vt)
            p.done("TE", TE.matmul(psA[:, 1024:1024 + TS], onesrow_b[0:1, :],
                                   stat_b[0:1, 0:TS], start=True, stop=True))
            tt = p.done("TE", TE.matmul(psA[:, 1536:1536 + TS], onesrow_b[0:1, :],
                                        stat_b[0:1, TS:2 * TS], start=True, stop=True))
            p.wait(V, "TE", tt)
            for c in range(8):
                p.vbar(V.tensor_tensor(dst_c(c), stream_c(c, sl), psA[:, 1024:1024 + TS], OP.mult))
                p.done("V", V.tensor_tensor(dst_c(c), dst_c(c), psA[:, 1536:1536 + TS], OP.add))

        if n_phases >= 1:
            p.arena_reset()
            xpath = p.aB(16 * (TS + 3))
            zsilu = p.aB(16 * TS)
            xcb = p.aB(16 * TS)
            delta = p.aB(16 * TS)
            ub = p.aB(16 * TS)
            ymb = p.aB(16 * TS)
            a_exp = p.aB(8 * TS)
            bu_t = p.aB(TS)
            dump = p.aB(TS)
            pscan = p.aB(TS)
            sp32 = p.aF(TS)
            gcum = p.aF(TS)
            invg_t = p.aF(TS)
            dhi_t = p.aF(TS)
            zcor = p.aF(TS)
            bu_2 = [bu_t, x_tm_b[:, 512:768]]
            pscan_2 = [pscan, x_tm_b[:, 256:512]]
            dump_2 = [dump, x_tm_b[:, 0:256]]
            dhi_2 = [dhi_t, p.aF(TS)]
            invg_2 = [invg_t, p.aF(TS)]
            gcum_2 = [gcum, p.aF(TS)]
            zcor_2 = [zcor, p.aF(TS)]
            gdone = [0, 0]

            def xpath_c(c):
                return xpath[:, c * (TS + 3):(c + 1) * (TS + 3)]

            def chunk(buf, c):
                return buf[:, c * TS:(c + 1) * TS]

            dtlow_b = p.sb("dtlow_b", [64, TS], BF16)
            b_rows = p.sb("b_rows", [16, TS], BF16)
            c_rows = p.sb("c_rows", [16, TS], BF16)
            brep = p.sb("brep", [128, TS], BF16)
            crep = p.sb("crep", [128, TS], BF16)
            convdiag = p.aB(64 * 128)
            etiles = p.aB((NG // 2) * 128)
            arow = x_tm_f[:, 256:512]
            arow_b = p.sb("arow_b", [128, 128], BF16)
            pstate = p.sb("pstate", [128, NG], F32)
            zstate = p.sb("zstate", [128, NG], F32)
            zstate2 = x_tm_f[:, 0:NG]
            pst_stage = p.sb("pst_stage", [128, 8], F32)
            convb_col = p.sb("convb_col", [128, 16], F32)
            bdt_col = p.sb("bdt_col", [128, 16], F32)
            dpar_col = p.sb("dpar_col", [128, 16], F32)
            wx_b = p.sb("wx_b", [128, 16 * 128], BF16)
            wdt_b = p.sb("wdt_b", [64, DI], BF16)

            V.memset(pstate[:, :], 0.0)
            V.memset(zstate[:, :], 0.0)
            V.memset(zstate2[:, :], 0.0)
            for c in range(16):
                V.memset(xpath_c(c)[:, 0:3], 0.0)

            for half in range(2):
                p.wait(Q, "TE")
                td = p.done("Qd", Q.dma_start(arow[:, 0:128],
                                              dr["A_log_g"][half * 128:(half + 1) * 128, :]))
                p.wait(S, "Qd", td)
                p.wait(S, "TE")
                stk = p.done("S", S.activation(arow_b[:, 0:128], arow[:, 0:128], AF.Exp))
                p.wait(TE, "S", stk)
                p.wait(V, "S", stk)
                for gi in range(128):
                    g = half * 128 + gi
                    lp = g % 8
                    tt = p.done("TE", TE.matmul(psD[0:64, 0:128],
                                                ident[:, gi:gi + 1].broadcast_to([128, 64]),
                                                arow_b[:, 0:128],
                                                start=True, stop=True))
                    p.wait(V, "TE", tt)
                    poff = ((g % 16) // 8) * 64
                    slot = (g // 16) * 8 + (g % 8)
                    vt = p.done("V", V.tensor_tensor(
                        etiles[poff:poff + 64, slot * 128:(slot + 1) * 128],
                        psD[0:64, 0:128], negm64[:, lp * 128:(lp + 1) * 128], OP.mult))
                    p.wait(TE, "V", vt)

            convw_col = x_tm_f[:, 512:576]
            td = p.done("Qd", Q.dma_start(convw_col[:, :], dr["conv_w_l"]))
            p.wait(V, "Qd")
            for c in range(16):
                for j in range(4):
                    p.done("V", V.tensor_scalar(convdiag[:, (c * 4 + j) * 128:(c * 4 + j + 1) * 128],
                                                ident[:, :], convw_col[:, c * 4 + j:c * 4 + j + 1],
                                                None, OP.mult))
            for tgt, src in [(convb_col, dr["conv_b_l"]), (bdt_col, dr["b_dt_l"]),
                             (dpar_col, dr["D_param_l"])]:
                p.done("Qd", Q.dma_start(tgt[:, :], src))
            for c in range(16):
                td = p.done("Qd", Q.dma_start(wstage[:, 0:128], dr["W_xT"][c * 128:(c + 1) * 128, :]))
                p.wait(V, "Qd", td)
                vt = p.done("V", V.tensor_copy(wx_b[:, c * 128:(c + 1) * 128], wstage[:, 0:128]))
                p.wait(Q, "V", vt)
            for hh in range(2):
                td = p.done("Qd", Q.dma_start(wstage[0:64, 0:1024],
                                              dr["W_dtT"][:, hh * 1024:(hh + 1) * 1024]))
                p.wait(V, "Qd", td)
                vt = p.done("V", V.tensor_copy(wdt_b[:, hh * 1024:(hh + 1) * 1024],
                                               wstage[0:64, 0:1024]))
                p.wait(Q, "V", vt)

            for s in range(NSL):
                sl = slice(s * TS, (s + 1) * TS)
                ln_slice(s, xn_c)
                for m in range(32):
                    if m == 0:
                        sw2_pre(dr["W_in_s"], 0)
                    w = sw2_consume()
                    if m + 1 < 32:
                        sw2_pre(dr["W_in_s"], m + 1)
                    for k in range(8):
                        tt = p.done("TE", TE.matmul(psB[:, 0:TS], w[:, k * 128:(k + 1) * 128],
                                                    xn_c(k),
                                                    start=(k == 0), stop=(k == 7)))
                    if m < 16:
                        p.wait(V, "TE", tt)
                        p.done("V", V.tensor_copy(xpath_c(m)[:, 3:3 + TS], psB[:, 0:TS]))
                    else:
                        p.wait(S, "TE", tt)
                        stk = p.done("S", S.activation(chunk(zsilu, m - 16), psB[:, 0:TS], AF.Silu))
                        p.wait(TE, "S", stk)
                p.wait(TE, "V")
                for c in range(16):
                    for j in range(4):
                        off = 3 - j
                        tt = p.done("TE", TE.matmul(psB[:, 0:TS],
                                                    convdiag[:, (c * 4 + j) * 128:(c * 4 + j + 1) * 128],
                                                    xpath_c(c)[:, 3 - off:3 - off + TS],
                                                    start=(j == 0), stop=(j == 3)))
                    p.wait(S, "TE", tt)
                    stk = p.done("S", S.activation(chunk(xcb, c), psB[:, 0:TS], AF.Silu,
                                                   bias=convb_col[:, c:c + 1]))
                    p.wait(TE, "S", stk)
                p.wait(V, "TE")
                for c in range(16):
                    p.done("V", V.tensor_copy(xpath_c(c)[:, 0:3], xpath_c(c)[:, TS:TS + 3]))
                for k in range(16):
                    tt = p.done("TE", TE.matmul(psB[:, 0:TS], wx_b[:, k * 128:(k + 1) * 128],
                                                chunk(xcb, k), start=(k == 0), stop=(k == 15)))
                p.wait(V, "TE", tt)
                V.tensor_copy(dtlow_b[:, :], psB[64:128, 0:TS])
                V.tensor_copy(b_rows[:, :], psB[0:16, 0:TS])
                vt = p.done("V", V.tensor_copy(c_rows[:, :], psB[32:48, 0:TS]))
                p.wait(TE, "V", vt)
                tt = p.done("TE", TE.matmul(psB[:, 0:TS], sel16[:, :], b_rows[0:16, :],
                                            start=True, stop=True))
                tt2 = p.done("TE", TE.matmul(psC[:, 0:TS], sel16[:, :], c_rows[0:16, :],
                                             start=True, stop=True))
                p.wait(V, "TE", tt2)
                V.tensor_copy(brep[:, :], psB[:, 0:TS])
                vt = p.done("V", V.tensor_copy(crep[:, :], psC[:, 0:TS]))
                for m in range(16):
                    p.wait(TE, "V")
                    tt = p.done("TE", TE.matmul(psB[:, 0:TS], wdt_b[:, m * 128:(m + 1) * 128],
                                                dtlow_b[:, :], start=True, stop=True))
                    p.wait(S, "TE", tt)
                    p.sbar(S.activation(sp32[:, :], psB[:, 0:TS], AF.Exp,
                                        bias=bdt_col[:, m:m + 1]))
                    stk = p.done("S", S.activation(sp32[:, :], sp32[:, :], AF.Ln,
                                                   bias=ccol["one"][:, 0:1]))
                    p.wait(V, "S", stk)
                    vt = p.done("V", V.tensor_scalar(chunk(delta, m), sp32[:, :], 1e-4, 10.0,
                                                     OP.add, OP.min))
                    p.wait(TE, "V", vt)
                    p.wait(S, "V", vt)
                p.wait(V, "V")
                for c in range(16):
                    p.done("V", V.tensor_tensor(chunk(ub, c), chunk(delta, c), chunk(xcb, c),
                                                OP.mult))
                p.wait(TE, "V")
                for gh in range(32):
                    dci, hf = gh // 2, gh % 2
                    doff = hf * 64
                    g0 = gh * 8
                    for bat in range(2):
                        for gsub4 in range(4):
                            gsub = bat * 4 + gsub4
                            g = g0 + gsub
                            poff = ((g % 16) // 8) * 64
                            slot = (g // 16) * 8 + (g % 8)
                            tt = p.done("TE", TE.matmul(
                                psA[:, gsub4 * 512:gsub4 * 512 + TS],
                                etiles[poff:poff + 64, slot * 128:(slot + 1) * 128],
                                chunk(delta, dci)[doff:doff + 64, :], start=True, stop=True))
                        p.wait(S, "TE", tt)
                        stk = p.done("S", S.activation(
                            a_exp[:, bat * 4 * TS:(bat + 1) * 4 * TS],
                            psA.rearrange("p (b t) -> p b t", t=512)[:, :, 0:TS],
                            AF.Exp))
                        p.wait(TE, "S", stk)
                    p.wait(V, "S", stk)
                    for gsub in range(8):
                        g = g0 + gsub
                        par = gsub % 2
                        buv = bu_2[par]
                        psv = pscan_2[par]
                        dhv = dhi_2[par]
                        inv = invg_2[par]
                        gcv = gcum_2[par]
                        zcv = zcor_2[par]
                        duv = dump_2[par]
                        tt = p.done("TE", TE.matmul(psB[:, 0:TS],
                                                    sel64u[doff:doff + 64, gsub * 128:(gsub + 1) * 128],
                                                    chunk(ub, dci)[doff:doff + 64, :],
                                                    start=True, stop=True))
                        p.wait(V, "TE", tt)
                        p.wait(V, "G", gdone[par])
                        p.vbar(V.tensor_tensor(buv[:, :], psB[:, 0:TS], brep[:, :], OP.mult))
                        p.vbar(V.tensor_scalar(buv[:, :], buv[:, :], 10.0, -10.0,
                                               OP.min, OP.max))
                        p.vbar(V.tensor_tensor_scan(psv[:, :],
                                                    a_exp[:, gsub * TS:(gsub + 1) * TS],
                                                    buv[:, :], pstate[:, g:g + 1],
                                                    OP.mult, OP.add))
                        p.vbar(V.tensor_copy(pst_stage[:, 0:2], psv[:, TS - 2:TS]))
                        V.tensor_copy(pstate[:, g:g + 1], pst_stage[:, 1:2])
                        vt = p.tick["V"]
                        p.wait(G, "V", vt)
                        p.gbar(G.tensor_scalar(dhv[:, :], psv[:, :], -1.0, 20.0,
                                               OP.mult, OP.add))
                        gt = p.gbar(G.tensor_scalar(inv[:, :], dhv[:, :], 1.0, -40.0,
                                                    OP.mult, OP.add))
                        p.wait(V, "G", gt)
                        p.vbar(V.tensor_tensor_scan(gcv[:, :],
                                                    a_exp[:, gsub * TS:(gsub + 1) * TS],
                                                    dhv[:, :], zstate[:, g:g + 1],
                                                    OP.mult, OP.min))
                        p.vbar(V.tensor_tensor_scan(zcv[:, :],
                                                    a_exp[:, gsub * TS:(gsub + 1) * TS],
                                                    inv[:, :], zstate2[:, g:g + 1],
                                                    OP.mult, OP.max))
                        V.tensor_copy(zstate[:, g:g + 1], gcv[:, TS - 1:TS])
                        p.vbar(V.tensor_copy(zstate2[:, g:g + 1], zcv[:, TS - 1:TS]))
                        vt = p.tick["V"]
                        p.wait(G, "V", vt)
                        p.gbar(G.tensor_tensor(gcv[:, :], gcv[:, :], zcv[:, :], OP.add))
                        p.gbar(G.tensor_tensor(psv[:, :], psv[:, :], gcv[:, :], OP.add))
                        gt = p.gbar(G.tensor_tensor(duv[:, :], psv[:, :],
                                                    crep[:, :], OP.mult))
                        gdone[par] = gt
                        p.wait(TE, "G", gt)
                        tt = p.done("TE", TE.matmul(psC[0:64, 0:TS],
                                                    sel64y[:, gsub * 64:(gsub + 1) * 64],
                                                    duv[:, :],
                                                    start=(gsub == 0), stop=(gsub == 7)))
                    p.wait(V, "TE", tt)
                    p.done("V", V.tensor_copy(chunk(ymb, dci)[doff:doff + 64, :],
                                              psC[0:64, 0:TS]))
                for c in range(16):
                    p.vbar(V.scalar_tensor_tensor(sp32[:, :], chunk(xcb, c), dpar_col[:, c:c + 1],
                                                  chunk(ymb, c), OP.mult, OP.add))
                    p.vbar(V.tensor_scalar(sp32[:, :], sp32[:, :], 50.0, -50.0, OP.min, OP.max))
                    p.vbar(V.tensor_tensor(chunk(ymb, c), sp32[:, :], chunk(zsilu, c),
                                           OP.mult))
                if dbg:
                    for nm in list(dbg_h):
                        if nm.startswith(f"s{s}_"):
                            buf = {"xc": xcb, "delta": delta, "ub": ub, "ymb": ymb,
                                   "zs": zsilu, "xp": xpath, "pst": pstate,
                                   "aex": a_exp, "but": bu_t, "psc": pscan}[nm.split("_")[1]]
                            p.wait(V, "TE")
                            vt = p.done("V", V.tensor_copy(x_tm_f[:, 0:dbg[nm]],
                                                           buf[:, 0:dbg[nm]]))
                            p.wait(Q, "V", vt)
                            td = p.done("Qd", Q.dma_start(dbg_h[nm], x_tm_f[:, 0:dbg[nm]]))
                            Q.wait_ge(p.sem["Qd"], td)
                            p.wait(V, "Qd")
                for m in range(8):
                    for h in range(2):
                        b = 2 * m + h
                        if b == 0:
                            sw2_pre(dr["W_outa_s"], 0)
                        w = sw2_consume()
                        if b + 1 < 16:
                            sw2_pre(dr["W_outa_s"], b + 1)
                        for k8 in range(8):
                            k = h * 8 + k8
                            tt = p.done("TE", TE.matmul(psB[:, 0:TS],
                                                        w[:, k8 * 128:(k8 + 1) * 128],
                                                        chunk(ymb, k),
                                                        start=(k == 0), stop=(k == 15)))
                    p.wait(V, "TE", tt)
                    p.done("V", V.tensor_tensor(stream_c(m, sl), stream_c(m, sl),
                                                psB[:, 0:TS], OP.add))


        def ffn_phase(e0buf, e1buf, dump2, dump3, rowbuf, crow, browbuf01, colbuf):
            for s in range(NSL):
                sl = slice(s * TS, (s + 1) * TS)
                ln_slice(s, xn_c)
                sw2_pre(dr["gate_s"], 0, kw=64)
                w = sw2_consume()
                for k in range(8):
                    tt = p.done("TE", TE.matmul(psB[0:8, 0:TS], w[:, k * 8:(k + 1) * 8],
                                                xn_c(k),
                                                start=(k == 0), stop=(k == 7)))
                p.wait(V, "TE", tt)
                vt = p.done("V", V.tensor_copy(rowbuf[0:8, 0:TS], psB[0:8, 0:TS]))
                p.wait(TE, "V", vt)
                for it in range(TS // 128):
                    p.wait(TE, "V")
                    tt = p.done("TE", TE.transpose(psD[0:128, 0:8],
                                                   rowbuf[0:8, it * 128:(it + 1) * 128],
                                                   identf[0:8, 0:8]))
                    p.wait(V, "TE", tt)
                    p.done("V", V.tensor_copy(colbuf[:, it * 8:(it + 1) * 8],
                                              psD[0:128, 0:8]))
                for it in range(TS // 128):
                    lg = colbuf[:, it * 8:(it + 1) * 8]
                    base = 32 + it * 16
                    M1 = colbuf[:, base + 0:base + 1]
                    M2 = colbuf[:, base + 1:base + 2]
                    Z = colbuf[:, base + 2:base + 3]
                    rZ = colbuf[:, base + 3:base + 4]
                    eM2 = colbuf[:, base + 4:base + 5]
                    m0 = colbuf[:, base + 5:base + 6]
                    m1 = colbuf[:, base + 6:base + 7]
                    c0 = colbuf[:, base + 7:base + 8]
                    c1 = colbuf[:, base + 8:base + 9]
                    negM1 = colbuf[:, base + 9:base + 10]
                    lpb = colbuf[:, 96 + it * 8:96 + (it + 1) * 8]
                    p.vbar(V.tensor_reduce(M1, lg, mybir.AxisListType.X, OP.max))
                    p.vbar(V.tensor_scalar(lpb, lg, M1, None, OP.is_ge))
                    p.vbar(V.scalar_tensor_tensor(lpb, lpb, -1e30, lg, OP.mult, OP.add))
                    p.vbar(V.tensor_reduce(M2, lpb, mybir.AxisListType.X, OP.max))
                    p.vbar(V.tensor_scalar(negM1, M1, -1.0, None, OP.mult))
                    p.wait(S, "V")
                    stk = p.done("S", S.activation(colbuf[:, 224 + it * 8:224 + (it + 1) * 8],
                                                   lg, AF.Exp, bias=negM1, accum_out=Z))
                    p.wait(V, "S", stk)
                    p.vbar(V.reciprocal(rZ, Z))
                    p.vbar(V.tensor_tensor(eM2, M2, negM1, OP.add))
                    p.wait(S, "V")
                    stk = p.done("S", S.activation(eM2, eM2, AF.Exp))
                    p.wait(V, "S", stk)
                    p.vbar(V.tensor_scalar(m0, lg[:, 0:1], M1, None, OP.is_ge))
                    p.vbar(V.tensor_scalar(m1, lpb[:, 1:2], M2, None, OP.is_ge))
                    p.vbar(V.tensor_tensor(c0, rZ, m0, OP.mult))
                    p.vbar(V.tensor_tensor(eM2, eM2, rZ, OP.mult))
                    p.vbar(V.tensor_tensor(c1, eM2, m1, OP.mult))
                for it in range(TS // 128):
                    base = 32 + it * 16
                    p.wait(TE, "V")
                    tt = p.done("TE", TE.transpose(psD[0:2, 0:128],
                                                   colbuf[:, base + 7:base + 9],
                                                   identf[:, :]))
                    p.wait(V, "TE", tt)
                    p.done("V", V.tensor_copy(crow[0:2, it * 128:(it + 1) * 128],
                                              psD[0:2, 0:128]))
                p.wait(V, "V")
                vt = p.done("V", V.tensor_copy(browbuf01[0:2, 0:TS], crow[0:2, 0:TS]))
                p.wait(TE, "V", vt)
                p.done("TE", TE.matmul(psA[:, 1024:1024 + TS], pairsel[0:2, 0:128],
                                       browbuf01[0:2, 0:TS], start=True, stop=True))
                tt = p.done("TE", TE.matmul(psA[:, 1536:1536 + TS], pairsel[0:2, 128:256],
                                            browbuf01[0:2, 0:TS], start=True, stop=True))
                p.wait(V, "TE", tt)
                for m in range(8):
                    if m == 0:
                        sw2_pre(dr["w0_s"], 0)
                    w = sw2_consume()
                    sw2_pre(dr["w1_s"], m)
                    for k in range(8):
                        tt0 = p.done("TE", TE.matmul(psB[:, 0:TS], w[:, k * 128:(k + 1) * 128],
                                                     xn_c(k),
                                                     start=(k == 0), stop=(k == 7)))
                    p.wait(S, "TE", tt0)
                    stk = p.done("S", S.activation(e0buf[:, 0:TS], psB[:, 0:TS],
                                                   AF.Silu, bias=b0_col[:, m:m + 1]))
                    p.wait(TE, "S", stk)
                    w = sw2_consume()
                    if m + 1 < 8:
                        sw2_pre(dr["w0_s"], m + 1)
                    for k in range(8):
                        tt1 = p.done("TE", TE.matmul(psB[:, 0:TS], w[:, k * 128:(k + 1) * 128],
                                                     xn_c(k),
                                                     start=(k == 0), stop=(k == 7)))
                    p.wait(S, "TE", tt1)
                    stk = p.done("S", S.activation(e1buf[:, 0:TS], psB[:, 0:TS],
                                                   AF.Silu, bias=b1_col[:, m:m + 1]))
                    p.wait(TE, "S", stk)
                    p.wait(V, "S", stk)
                    p.vbar(V.tensor_tensor(dump2[:, :], e0buf[:, 0:TS],
                                           psA[:, 1024:1024 + TS], OP.mult))
                    p.vbar(V.tensor_tensor(dump3[:, :], e1buf[:, 0:TS],
                                           psA[:, 1536:1536 + TS], OP.mult))
                    p.vbar(V.tensor_tensor(dump2[:, :], dump2[:, :], dump3[:, :], OP.add))
                    p.done("V", V.tensor_tensor(stream_c(m, sl), stream_c(m, sl),
                                                dump2[:, :], OP.add))

        if n_phases >= 2:
            p.arena_reset()
            e0buf = p.aB(TS)
            e1buf = p.aB(TS)
            dump2 = p.aF(TS)
            dump3 = p.aF(TS)
            rowbuf = p.sb("rowbuf", [8, 2 * TS], F32)
            crow = p.sb("crow", [2, TS], F32)
            browbuf01 = p.sb("browbuf01", [2, TS], BF16)
            colbuf = p.sb("colbuf", [128, 256], F32)
            ffn_phase(e0buf, e1buf, dump2, dump3, rowbuf, crow, browbuf01, colbuf)


        if n_phases >= 3:
            p.arena_reset()
            NC8 = TS + 3
            updb = p.aB(8 * TS)
            cndb = p.aB(8 * TS)
            cab = p.aB(4 * TS)
            sab_ = p.aB(4 * TS)
            qbuf = p.aB(8 * NC8)
            kbuf = p.aB(8 * NC8)
            vbuf = p.aB(8 * NC8)
            ogb = p.aB(8 * TS)
            ment = p.aB(6 * 4 * TS)
            mtmp = p.aB(7 * 2 * TS)
            vecs = p.aB(8 * TS)
            reads_tm = p.aB(2 * 1024)
            mvfm = p.aB(16 * TS)
            cdiag3 = p.aB(32 * 128)
            dmA = p.aB(64); dgG = p.aB(64); nmat = p.aB(64); gmat = p.aB(64)
            ktm = p.aB(64); vtm = p.aB(64); xsol = p.aB(64)
            bdn = p.aB(128); bdnt = p.aB(128); bdn2 = p.aB(128); bdnt2 = p.aB(128)
            tmk = p.aB(128)
            scident = p.aB(128)
            sqtile = p.aB(1024)
            angf = p.aF(4 * TS)
            sp32 = p.aF(TS)
            dump2 = p.aF(TS)
            shalf = p.aF(TS)
            chalf = p.aF(TS)
            gatef = p.aF(TS)
            e1f = p.aF(64)

            def updc(c):
                return updb[:, c * TS:(c + 1) * TS]

            def qc(buf, c):
                return buf[:, c * NC8:(c + 1) * NC8]

            def me(i, c):
                return ment[:, (i * 4 + c) * TS:(i * 4 + c) * TS + TS]

            scum16 = p.sb("scum16", [16, TS], F32)
            spn16 = p.sb("spn16", [16, TS], F32)

            ab32 = p.sb("ab32", [128, TS], F32)
            alpha16 = p.sb("alpha16", [16, TS], F32)
            beta16 = p.sb("beta16", [16, TS], F32)
            banr = p.sb("banr", [16, TS], F32)
            scstate = x_tm_f[0:16, 944:945]
            scprev = x_tm_f[0:16, 945:946]
            scum_row = p.sb("scum_row", [1, 16 * 64], F32)
            batchA = x_tm_f[:, 946:954]
            tmc = x_tm_f[0:64, 954:970]
            batchR = x_tm_f[:, 272:280]
            batchS = x_tm_f[:, 280:288]
            batchG = x_tm_f[:, 288:296]
            batchGB = x_tm_f[:, 296:304]
            batchBA = x_tm_f[:, 304:312]
            batchB = x_tm_f[:, 312:320]
            row16 = x_tm_f[0:16, 320:384]
            row16b = x_tm_f[0:16, 384:448]
            row16t = x_tm_f[0:16, 0:TS]
            row16tb = x_tm_b[0:16, 0:TS]
            gl16 = x_tm_f[0:1, TS:TS + 16]
            hsum = x_tm_f[:, 512:528]
            paircol = x_tm_f[:, 528:529]
            qcw_col = x_tm_f[:, 529:625]
            s0p = x_tm_b[:, 256:768]
            vstate = x_tm_f[:, 936:944]
            mbias_a = x_tm_f[:, 640:704]
            mbias_g = x_tm_f[:, 704:768]
            wblk_col = x_tm_f[:, 768:784]
            rmsm_col = x_tm_f[:, 784:792]
            rmsv_col = x_tm_f[:, 792:800]
            boutb_col = x_tm_f[:, 800:808]
            bctrl_col = x_tm_f[:, 808:861]
            onesf_row = x_tm_f[0:1, 861:925]
            cmhalfpi = x_tm_f[:, 925:926]
            stg = x_tm_f[:, 928:932]
            stg16 = x_tm_f[0:16, 932:936]

            for tgt, src in [(mbias_a, dr["MBIAS_A"]), (mbias_g, dr["MBIAS_G"]),
                             (wblk_col, dr["W_block_l"]),
                             (rmsm_col, dr["rms_m_l"]), (rmsv_col, dr["rms_v_perm"]),
                             (boutb_col, dr["b_outb_l"]), (bctrl_col, dr["b_ctrl_cols"])]:
                p.done("Qd", Q.dma_start(tgt[:, :], src))
            p.done("Qd", Q.dma_start(qcw_col[:, 0:32], dr["qconv_l"]))
            p.done("Qd", Q.dma_start(qcw_col[:, 32:64], dr["kconv_l"]))
            p.done("Qd", Q.dma_start(qcw_col[:, 64:96], dr["vconv_l"]))
            p.wait(V, "Qd")
            V.memset(onesf_row[0:1, :], 1.0)
            V.memset(cmhalfpi[:, :], -np.pi / 2)
            V.memset(s0p[:, :], 0.0)
            V.memset(vstate[:, :], 0.0)
            V.memset(scstate[:, :], 0.0)
            V.memset(scprev[:, :], 0.0)
            for c in range(8):
                V.memset(qc(qbuf, c)[:, 0:3], 0.0)
                V.memset(qc(kbuf, c)[:, 0:3], 0.0)
                p.done("V", V.memset(qc(vbuf, c)[:, 0:3], 0.0))

            def compose(dst_t0, dst_step, src_t0, src_step, n, bar):
                ME = ment.rearrange("p (i c t) -> p i c t", i=6, c=4)
                A = [ME[:, i, :, dst_t0:dst_t0 + dst_step * (n - 1) + 1:dst_step]
                     for i in range(6)]
                B = [ME[:, i, :, src_t0:src_t0 + src_step * (n - 1) + 1:src_step]
                     for i in range(6)]
                MT = mtmp.rearrange("p (i t) -> p i t", i=7)
                Tt = [MT[:, i, 0:4 * n].rearrange("p (c t) -> p c t", c=4) for i in range(7)]

                def op(ins):
                    if bar:
                        p.vbar(ins)
                    else:
                        p.done("V", ins)

                for (o, a0, b0, a1, b1) in ((0, 0, 0, 1, 2), (1, 0, 1, 1, 3),
                                            (2, 2, 0, 3, 2), (3, 2, 1, 3, 3),
                                            (4, 0, 4, 1, 5), (5, 2, 4, 3, 5)):
                    op(V.tensor_tensor(Tt[o], A[a0], B[b0], OP.mult))
                    op(V.tensor_tensor(Tt[6], A[a1], B[b1], OP.mult))
                    op(V.tensor_tensor(Tt[o], Tt[o], Tt[6], OP.add))
                    if o >= 4:
                        op(V.tensor_tensor(Tt[o], Tt[o], A[o], OP.add))
                p.vbar(V.tensor_copy(Tt[6][:, :, 0:n] if False else MT[:, 6, 0:4 * n],
                                     MT[:, 6, 0:4 * n]))
                for i in range(6):
                    op(V.tensor_copy(A[i], Tt[i]))
                p.vbar(V.tensor_copy(MT[:, 6, 0:4], MT[:, 6, 0:4]))

            for s in range(NSL):
                sl = slice(s * TS, (s + 1) * TS)
                ln_slice(s, xn_c)
                for m in range(53):
                    if m == 0:
                        sw2_pre(dr["W_ctrl_s"], 0)
                    w = sw2_consume()
                    if m + 1 < 53:
                        sw2_pre(dr["W_ctrl_s"], m + 1)
                    for k in range(8):
                        tt = p.done("TE", TE.matmul(psB[:, 0:TS], w[:, k * 128:(k + 1) * 128],
                                                    xn_c(k),
                                                    start=(k == 0), stop=(k == 7)))
                    bcol = bctrl_col[:, m:m + 1]
                    if m < 8:
                        p.wait(S, "TE", tt)
                        stk = p.done("S", S.activation(updc(m), psB[:, 0:TS], AF.Sigmoid,
                                                       bias=bcol))
                        p.wait(TE, "S", stk)
                    elif m < 12:
                        p.wait(V, "TE", tt)
                        vt = p.done("V", V.tensor_scalar(angf[:, (m - 8) * TS:(m - 7) * TS],
                                                         psB[:, 0:TS], bcol, None, OP.add))
                        p.wait(TE, "V", vt)
                    elif m < 20:
                        p.wait(S, "TE", tt)
                        stk = p.done("S", S.activation(cndb[:, (m - 12) * TS:(m - 11) * TS],
                                                       psB[:, 0:TS], AF.Silu, bias=bcol))
                        p.wait(TE, "S", stk)
                    elif m < 44:
                        buf = qbuf if m < 28 else (kbuf if m < 36 else vbuf)
                        cc = (m - 20) % 8
                        p.wait(V, "TE", tt)
                        vt = p.done("V", V.tensor_scalar(qc(buf, cc)[:, 3:3 + TS],
                                                         psB[:, 0:TS], bcol, None, OP.add))
                        p.wait(TE, "V", vt)
                    elif m == 44:
                        p.wait(V, "TE", tt)
                        vt = p.done("V", V.tensor_scalar(ab32[:, :], psB[:, 0:TS], bcol,
                                                         None, OP.add))
                        p.wait(TE, "V", vt)
                    else:
                        p.wait(V, "TE", tt)
                        vt = p.done("V", V.tensor_scalar(ogb[:, (m - 45) * TS:(m - 44) * TS],
                                                         psB[:, 0:TS], bcol, None, OP.add))
                        p.wait(TE, "V", vt)
                p.wait(S, "V")
                S.activation(alpha16[:, :], ab32[0:16, :], AF.Sigmoid)
                stk = p.done("S", S.activation(beta16[:, :], ab32[64:80, :], AF.Sigmoid))
                p.sbar(S.activation(spn16[:, :], ab32[0:16, :], AF.Exp, scale=-1.0))
                stk = p.done("S", S.activation(spn16[:, :], spn16[:, :], AF.Ln,
                                               bias=ccol["one"][0:16, 0:1]))
                p.wait(V, "S", stk)
                p.vbar(V.tensor_tensor_scan(scum16[:, :],
                                            ccol["zero"][0:16, 0:1].broadcast_to([16, TS]),
                                            spn16[:, :], scstate[0:16, 0:1], OP.add, OP.add))
                p.vbar(V.tensor_copy(stg16[0:16, 0:2], scum16[:, TS - 2:TS]))
                p.vbar(V.tensor_copy(scstate[0:16, 0:1], stg16[0:16, 1:2]))
                p.vbar(V.reciprocal(banr[:, :], alpha16[:, :]))
                p.vbar(V.tensor_tensor(banr[:, :], banr[:, :], beta16[:, :], OP.mult))
                p.vbar(V.tensor_scalar(banr[:, :], banr[:, :], -1.0, None, OP.mult))
                for which, buf in ((0, qbuf), (1, kbuf), (2, vbuf)):
                    p.wait(V, "S")
                    p.wait(V, "TE")
                    for c in range(8):
                        for j in range(4):
                            p.done("V", V.tensor_scalar(
                                cdiag3[:, (c * 4 + j) * 128:(c * 4 + j + 1) * 128],
                                ident[:, :], qcw_col[:, which * 32 + c * 4 + j:
                                                     which * 32 + c * 4 + j + 1],
                                None, OP.mult))
                    p.wait(TE, "V")
                    for c in range(8):
                        for j in range(4):
                            off = 3 - j
                            tt = p.done("TE", TE.matmul(
                                psB[:, 0:TS],
                                cdiag3[:, (c * 4 + j) * 128:(c * 4 + j + 1) * 128],
                                qc(buf, c)[:, 3 - off:3 - off + TS],
                                start=(j == 0), stop=(j == 3)))
                        p.wait(V, "TE", tt)
                        p.vbar(V.tensor_copy(stg[:, 0:3], qc(buf, c)[:, TS:TS + 3]))
                        vt = p.done("V", V.tensor_copy(qc(buf, c)[:, 0:3], stg[:, 0:3]))
                        p.wait(S, "V", vt)
                        stk = p.done("S", S.activation(qc(buf, c)[:, 3:3 + TS], psB[:, 0:TS],
                                                       AF.Silu))
                        p.wait(TE, "S", stk)
                        p.wait(V, "S", stk)
                for which, buf in ((0, qbuf), (1, kbuf)):
                    for c in range(8):
                        p.wait(S, "TE")
                        stk = p.done("S", S.activation(x2b[:, :], qc(buf, c)[:, 3:3 + TS],
                                                       AF.Square))
                        p.wait(TE, "S", stk)
                        tt = p.done("TE", TE.matmul(psC[0:16, 0:TS],
                                                    selq8[:, c * 16:(c + 1) * 16],
                                                    x2b[:, :], start=(c == 0), stop=(c == 7)))
                    p.wait(V, "TE", tt)
                    p.vbar(V.tensor_scalar(row16t[:, 0:TS], psC[0:16, 0:TS], 1.0, 1e-12,
                                           OP.mult, OP.add))
                    vt = p.tick["V"]
                    p.wait(S, "V", vt)
                    p.sbar(S.activation(row16t[0:16, 0:TS], row16t[0:16, 0:TS], AF.Sqrt))
                    p.wait(V, "S")
                    p.vbar(V.reciprocal(row16t[:, 0:TS], row16t[:, 0:TS]))
                    vt = p.done("V", V.tensor_copy(row16tb[:, 0:TS], row16t[:, 0:TS]))
                    p.wait(TE, "V", vt)
                    for c in range(8):
                        tt = p.done("TE", TE.matmul(psC[:, 0:TS],
                                                    selqe[:, c * 128:(c + 1) * 128],
                                                    row16tb[:, 0:TS], start=True, stop=True))
                        p.wait(V, "TE", tt)
                        p.vbar(V.tensor_tensor(qc(buf, c)[:, 3:3 + TS],
                                               qc(buf, c)[:, 3:3 + TS], psC[:, 0:TS],
                                               OP.mult))
                        p.wait(TE, "V")
                for c in range(4):
                    ac = angf[:, c * TS:(c + 1) * TS]
                    p.vbar(V.tensor_scalar(sp32[:, :], ac, 1.0 / (2 * np.pi), None, OP.mult))
                    p.vbar(V.tensor_scalar(dump2[:, :], sp32[:, :], 0.5, None, OP.is_ge))
                    for kthr in (1.5, 2.5):
                        p.vbar(V.tensor_scalar(gatef[:, :], sp32[:, :], kthr, None,
                                               OP.is_ge))
                        p.vbar(V.tensor_tensor(dump2[:, :], dump2[:, :], gatef[:, :],
                                               OP.add))
                    for kthr in (-0.5, -1.5, -2.5):
                        p.vbar(V.tensor_scalar(gatef[:, :], sp32[:, :], kthr, None,
                                               OP.is_le))
                        p.vbar(V.tensor_tensor(dump2[:, :], dump2[:, :], gatef[:, :],
                                               OP.subtract))
                    p.vbar(V.scalar_tensor_tensor(sp32[:, :], dump2[:, :], -2 * np.pi, ac,
                                                  OP.mult, OP.add))
                    p.vbar(V.tensor_scalar(sp32[:, :], sp32[:, :], np.pi, -np.pi,
                                           OP.min, OP.max))
                    p.vbar(V.tensor_scalar(dump2[:, :], sp32[:, :], 0.5, -np.pi / 2,
                                           OP.mult, OP.add))
                    vt = p.tick["V"]
                    p.wait(S, "V", vt)
                    S.activation(shalf[:, :], dump2[:, :], AF.Sin)
                    stk = p.done("S", S.activation(chalf[:, :], sp32[:, :], AF.Sin, scale=0.5))
                    p.wait(V, "S", stk)
                    p.vbar(V.tensor_tensor(sp32[:, :], chalf[:, :], chalf[:, :], OP.mult))
                    p.vbar(V.tensor_scalar(cab[:, c * TS:(c + 1) * TS], sp32[:, :], -2.0, 1.0,
                                           OP.mult, OP.add))
                    p.vbar(V.tensor_tensor(sp32[:, :], chalf[:, :], shalf[:, :], OP.mult))
                    p.vbar(V.tensor_scalar(sab_[:, c * TS:(c + 1) * TS], sp32[:, :], -2.0,
                                           None, OP.mult))
                for c in range(4):
                    cav = cab[:, c * TS:(c + 1) * TS]
                    sav = sab_[:, c * TS:(c + 1) * TS]
                    u1 = updc(c)
                    u2 = updc(4 + c)
                    wc4 = [wblk_col[:, c * 4 + i:c * 4 + i + 1] for i in range(4)]
                    for (o, wa, wb, uu, sgn) in ((0, 0, 1, u1, -1.0), (1, 2, 3, u1, -1.0),
                                                 (2, 0, 1, u2, 1.0), (3, 2, 3, u2, 1.0)):
                        if o < 2:
                            p.vbar(V.tensor_scalar(sp32[:, :], cav, wc4[wa], None, OP.mult))
                            p.vbar(V.tensor_scalar(dump2[:, :], sav, wc4[wb], None, OP.mult))
                            p.vbar(V.tensor_tensor(sp32[:, :], sp32[:, :], dump2[:, :],
                                                   OP.subtract))
                        else:
                            p.vbar(V.tensor_scalar(sp32[:, :], sav, wc4[wa], None, OP.mult))
                            p.vbar(V.tensor_scalar(dump2[:, :], cav, wc4[wb], None, OP.mult))
                            p.vbar(V.tensor_tensor(sp32[:, :], sp32[:, :], dump2[:, :],
                                                   OP.add))
                        p.vbar(V.tensor_tensor(me(o, c), uu, sp32[:, :], OP.mult))
                    p.vbar(V.tensor_scalar(sp32[:, :], u1, -1.0, 1.0, OP.mult, OP.add))
                    p.vbar(V.tensor_tensor(me(4, c), sp32[:, :],
                                           cndb[:, c * TS:(c + 1) * TS], OP.mult))
                    p.vbar(V.tensor_scalar(sp32[:, :], u2, -1.0, 1.0, OP.mult, OP.add))
                    p.vbar(V.tensor_tensor(me(5, c), sp32[:, :],
                                           cndb[:, (4 + c) * TS:(5 + c) * TS], OP.mult))
                lev = 1
                while 2 * lev <= TS:
                    n = TS // (2 * lev)
                    compose(2 * lev - 1, 2 * lev, lev - 1, 2 * lev, n, bar=True)
                    lev *= 2
                lev = TS // 4
                while lev >= 1:
                    n = TS // (2 * lev) - 1
                    if n > 0:
                        compose(3 * lev - 1, 2 * lev, 2 * lev - 1, 2 * lev, n, bar=True)
                    lev //= 2
                for c in range(4):
                    v1c = vstate[:, c:c + 1]
                    v2c = vstate[:, 4 + c:4 + c + 1]
                    p.vbar(V.tensor_scalar(sp32[:, :], me(0, c), v1c, None, OP.mult))
                    p.vbar(V.scalar_tensor_tensor(dump2[:, :], me(1, c), v2c, sp32[:, :],
                                                  OP.mult, OP.add))
                    p.vbar(V.tensor_tensor(vecs[:, c * TS:(c + 1) * TS], dump2[:, :],
                                           me(4, c), OP.add))
                    p.vbar(V.tensor_scalar(sp32[:, :], me(2, c), v1c, None, OP.mult))
                    p.vbar(V.scalar_tensor_tensor(dump2[:, :], me(3, c), v2c, sp32[:, :],
                                                  OP.mult, OP.add))
                    p.vbar(V.tensor_tensor(vecs[:, (4 + c) * TS:(5 + c) * TS], dump2[:, :],
                                           me(5, c), OP.add))
                for c in range(8):
                    p.vbar(V.tensor_copy(stg[:, 0:2],
                                         vecs[:, c * TS + TS - 2:c * TS + TS]))
                    p.vbar(V.tensor_copy(vstate[:, c:c + 1], stg[:, 1:2]))

                for c in range(4):
                    rb = scum16[:, c * 64 - 1:c * 64] if c > 0 else scprev[0:16, 0:1]
                    p.vbar(V.tensor_scalar(row16[:, 0:64], scum16[:, c * 64:(c + 1) * 64],
                                           rb, None, OP.subtract))
                    p.vbar(V.tensor_scalar(row16b[:, 0:1], row16[:, 63:64], -1.0, None,
                                           OP.mult))
                    vt = p.tick["V"]
                    p.wait(S, "V", vt)
                    stk = p.done("S", S.activation(row16b[:, 0:1], row16b[:, 0:1], AF.Exp))
                    p.wait(TE, "S", stk)
                    tt = p.done("TE", TE.transpose(psD[0:1, 0:16], row16b[:, 0:1],
                                                   identf[0:16, 0:16]))
                    p.wait(V, "TE", tt)
                    p.vbar(V.tensor_copy(gl16[0:1, 0:16], psD[0:1, 0:16]))
                    for (rows, dstb) in ((scum16[:, c * 64:(c + 1) * 64], batchA),
                                         (row16[:, 0:64], batchR),
                                         (spn16[:, c * 64:(c + 1) * 64], batchS),
                                         (banr[:, c * 64:(c + 1) * 64], batchBA),
                                         (beta16[:, c * 64:(c + 1) * 64], batchB)):
                        p.wait(TE, "V")
                        tt = p.done("TE", TE.transpose(psD[0:64, 0:16], rows[0:16, :],
                                                       identf[0:16, 0:16]))
                        p.wait(V, "TE", tt)
                        vt = p.done("V", V.tensor_copy(tmc[0:64, 0:16], psD[0:64, 0:16]))
                        p.wait(TE, "V", vt)
                        TE.matmul(psC[0:64, 0:8], identf[0:64, 0:64], tmc[0:64, 0:16:2],
                                  start=True, stop=True)
                        tt = p.done("TE", TE.matmul(psC[64:128, 0:8], identf[0:64, 0:64],
                                                    tmc[0:64, 1:16:2], start=True, stop=True))
                        p.wait(V, "TE", tt)
                        p.vbar(V.tensor_copy(dstb[:, 0:8], psC[:, 0:8]))
                        if dstb is batchA:
                            for h in range(16):
                                p.wait(TE, "V")
                                tt = p.done("TE", TE.transpose(psD[0:1, 0:64],
                                                               tmc[0:64, h:h + 1],
                                                               identf[0:64, 0:64]))
                                p.wait(V, "TE", tt)
                                p.done("V", V.tensor_copy(scum_row[0:1, h * 64:(h + 1) * 64],
                                                          psD[0:1, 0:64]))
                            p.vbar(V.tensor_copy(scum_row[0:1, 0:2], scum_row[0:1, 0:2]))
                    p.vbar(V.tensor_scalar(batchG[:, :], batchR[:, :], -1.0, None, OP.mult))
                    p.vbar(V.tensor_tensor(batchGB[:, :], batchS[:, :], batchR[:, :],
                                           OP.subtract))
                    vt = p.tick["V"]
                    p.wait(S, "V", vt)
                    S.activation(batchG[:, :], batchG[:, :], AF.Exp)
                    stk = p.done("S", S.activation(batchGB[:, :], batchGB[:, :], AF.Exp))
                    p.wait(V, "S", stk)
                    p.wait(TE, "S", stk)
                    for j in range(8):
                        hc1, ho1 = j, 0
                        hc2, ho2 = j, 64
                        bo = j * 64
                        p.wait(TE, "V")
                        TE.matmul(psD[0:64, 0:64], onesf_row[0:1, 0:64],
                                  scum_row[0:1, (2 * j) * 64:(2 * j + 1) * 64],
                                  start=True, stop=True)
                        tt = p.done("TE", TE.matmul(psD[64:128, 0:64], onesf_row[0:1, 0:64],
                                                    scum_row[0:1, (2 * j + 1) * 64:(2 * j + 2) * 64],
                                                    start=True, stop=True))
                        p.wait(V, "TE", tt)
                        p.vbar(V.tensor_tensor(e1f[:, 0:64], psD[:, 0:64],
                                               batchA[:, j:j + 1].broadcast_to([128, 64]),
                                               OP.subtract))
                        p.vbar(V.tensor_tensor(dump2[:, 0:64], e1f[:, 0:64], mbias_a[:, :],
                                               OP.add))
                        p.vbar(V.scalar_tensor_tensor(sp32[:, 0:64], e1f[:, 0:64], -1.0,
                                                      mbias_g[:, :], OP.mult, OP.add))
                        vt = p.tick["V"]
                        p.wait(S, "V", vt)
                        S.activation(dmA[:, :], dump2[:, 0:64], AF.Exp)
                        stk = p.done("S", S.activation(dgG[:, :], sp32[:, 0:64], AF.Exp))
                        p.wait(V, "S", stk)
                        kap1 = qc(kbuf, j)[0:64, 3 + c * 64:3 + (c + 1) * 64]
                        kap2 = qc(kbuf, j)[64:128, 3 + c * 64:3 + (c + 1) * 64]
                        qap1 = qc(qbuf, j)[0:64, 3 + c * 64:3 + (c + 1) * 64]
                        qap2 = qc(qbuf, j)[64:128, 3 + c * 64:3 + (c + 1) * 64]
                        p.wait(TE, "V")
                        TE.matmul(psA[0:64, 0:64], kap1, kap1, start=True, stop=True)
                        tt = p.done("TE", TE.matmul(psA[64:128, 0:64], kap2, kap2,
                                                    start=True, stop=True))
                        TE.matmul(psA[0:64, 1024:1088], kap1, qap1, start=True, stop=True)
                        tt = p.done("TE", TE.matmul(psA[64:128, 1024:1088], kap2, qap2,
                                                    start=True, stop=True))
                        p.wait(V, "TE", tt)
                        p.vbar(V.tensor_tensor(nmat[:, :], dmA[:, :], psA[:, 0:64],
                                               OP.mult))
                        p.vbar(V.tensor_scalar(nmat[:, :], nmat[:, :], batchBA[:, j:j + 1],
                                               None, OP.mult))
                        p.vbar(V.tensor_tensor(gmat[:, :], dgG[:, :],
                                               psA[:, 1024:1088], OP.mult))
                        for (srcb, dstt) in ((kbuf, ktm), (vbuf, vtm)):
                            p.wait(TE, "V")
                            tt = p.done("TE", TE.transpose(
                                psE[0:64, 0:128],
                                qc(srcb, j)[:, 3 + c * 64:3 + (c + 1) * 64], ident[:, :]))
                            p.wait(V, "TE", tt)
                            vt = p.done("V", V.tensor_copy(tmk[0:64, :], psE[0:64, 0:128]))
                            if dbg and s == 0 and c == 0 and j == 0 and "mach" in dbg \
                                    and dstt is ktm and dbg["mach"] > 544:
                                vt = p.done("V", V.tensor_copy(wstage[0:64, 640:768],
                                                               tmk[0:64, :]))
                            p.wait(TE, "V", vt)
                            TE.matmul(psC[0:64, 0:64], ident[0:64, 0:64], tmk[0:64, 0:64],
                                      start=True, stop=True)
                            tt = p.done("TE", TE.matmul(psC[64:128, 0:64], ident[0:64, 0:64],
                                                        tmk[0:64, 64:128],
                                                        start=True, stop=True))
                            p.wait(V, "TE", tt)
                            p.vbar(V.tensor_copy(dstt[:, :], psC[:, 0:64]))
                        p.wait(TE, "V")
                        TE.matmul(psC[0:64, 0:64],
                                  qc(kbuf, hc1)[ho1:ho1 + 64, 3 + c * 64:3 + (c + 1) * 64],
                                  s0p[0:64, j * 64:(j + 1) * 64], start=True, stop=True)
                        tt = p.done("TE", TE.matmul(
                            psC[64:128, 0:64],
                            qc(kbuf, hc2)[ho2:ho2 + 64, 3 + c * 64:3 + (c + 1) * 64],
                            s0p[64:128, j * 64:(j + 1) * 64], start=True, stop=True))
                        p.wait(V, "TE", tt)
                        p.vbar(V.tensor_scalar(dump2[:, 0:64], psC[:, 0:64],
                                               batchGB[:, j:j + 1], None, OP.mult))
                        p.vbar(V.tensor_tensor(dump2[:, 0:64], vtm[:, :], dump2[:, 0:64],
                                               OP.subtract))
                        p.vbar(V.tensor_scalar(xsol[:, :], dump2[:, 0:64],
                                               batchB[:, j:j + 1], None, OP.mult))
                        p.vbar(V.memset(bdn[:, :], 0.0))
                        V.tensor_copy(bdn[0:64, 0:64], nmat[0:64, :])
                        p.vbar(V.tensor_copy(bdn[64:128, 64:128], nmat[64:128, :]))
                        p.wait(TE, "V")
                        tt = p.done("TE", TE.transpose(psE[:, 0:128], bdn[:, :], ident[:, :]))
                        p.wait(V, "TE", tt)
                        p.vbar(V.tensor_copy(bdnt[:, :], psE[:, 0:128]))
                        pw, pwt = bdn, bdnt
                        alt, altt = bdn2, bdnt2
                        for ksq in range(6):
                            p.wait(TE, "V")
                            TE.matmul(psC[:, 0:64], pwt[:, :], xsol[:, :],
                                      start=True, stop=False)
                            tt = p.done("TE", TE.matmul(psC[:, 0:64], ident[:, :],
                                                        xsol[:, :], start=False, stop=True))
                            p.wait(S, "TE", tt)
                            stk = p.done("S", S.activation(xsol[:, :], psC[:, 0:64], AF.Copy))
                            p.wait(TE, "S", stk)
                            p.wait(V, "S", stk)
                            if ksq < 5:
                                TE.matmul(psC[:, 0:128], pwt[:, :], pw[:, :],
                                          start=True, stop=True)
                                tt = p.done("TE", TE.matmul(psD[:, 0:128], pw[:, :],
                                                            pwt[:, :], start=True, stop=True))
                                p.wait(V, "TE", tt)
                                p.vbar(V.tensor_copy(alt[:, :], psC[:, 0:128]))
                                p.vbar(V.tensor_copy(altt[:, :], psD[:, 0:128]))
                                pw, alt = alt, pw
                                pwt, altt = altt, pwt
                        if dbg and s == 0 and c == 0 and j == 0 and "mach" in dbg:
                            p.wait(V, "TE")
                            p.wait(V, "S")
                            for off, bb in ((0, nmat), (64, gmat), (128, xsol), (192, ktm),
                                            (256, vtm), (384, dmA), (448, dgG)):
                                p.done("V", V.tensor_copy(wstage[:, off:off + 64], bb[:, :]))
                            vt = p.done("V", V.tensor_copy(wstage[:, 320:384], e1f[:, 0:64]))
                            vt = p.done("V", V.tensor_copy(wstage[:, 512:520], batchA[:, 0:8]))
                            vt = p.done("V", V.tensor_copy(wstage[:, 520:528], batchR[:, 0:8]))
                            vt = p.done("V", V.tensor_copy(wstage[:, 528:536], batchGB[:, 0:8]))
                            vt = p.done("V", V.tensor_copy(wstage[:, 536:544], batchBA[:, 0:8]))
                            if dbg["mach"] > 544:
                                p.done("V", V.tensor_copy(wstage[:, 576:640],
                                                          qc(kbuf, 0)[:, 3:67]))
                                p.done("V", V.tensor_copy(wstage[0:64, 768:784],
                                                          tmc[0:64, 0:16]))
                                p.done("V", V.tensor_copy(wstage[0:1, 784:912],
                                                          scum_row[0:1, 0:128]))
                                p.done("V", V.tensor_copy(wstage[:, 912:976],
                                                          psA[:, 0:64]))
                                vt = p.done("V", V.tensor_copy(wstage[:, 976:1024],
                                                               psA[:, 1024:1072]))
                            p.wait(Q, "V", vt)
                            td = p.done("Qd", Q.dma_start(dbg_h["mach"], wstage[:, 0:dbg["mach"]]))
                            Q.wait_ge(p.sem["Qd"], td)
                            p.wait(V, "Qd")
                        p.wait(TE, "V")
                        TE.matmul(psC[0:64, 0:64], gmat[0:64, :], xsol[0:64, :],
                                  start=True, stop=True)
                        tt = p.done("TE", TE.matmul(psC[64:128, 0:64], gmat[64:128, :],
                                                    xsol[64:128, :], start=True, stop=True))
                        TE.matmul(psD[0:64, 0:64],
                                  qc(qbuf, hc1)[ho1:ho1 + 64, 3 + c * 64:3 + (c + 1) * 64],
                                  s0p[0:64, j * 64:(j + 1) * 64], start=True, stop=True)
                        tt2 = p.done("TE", TE.matmul(
                            psD[64:128, 0:64],
                            qc(qbuf, hc2)[ho2:ho2 + 64, 3 + c * 64:3 + (c + 1) * 64],
                            s0p[64:128, j * 64:(j + 1) * 64], start=True, stop=True))
                        p.wait(V, "TE", tt2)
                        p.vbar(V.tensor_scalar(dump2[:, 0:64], psD[:, 0:64],
                                               batchG[:, j:j + 1], None, OP.mult))
                        p.vbar(V.tensor_tensor(dump2[:, 0:64], dump2[:, 0:64], psC[:, 0:64],
                                               OP.add))
                        ti2 = (c * 64) // 128
                        tof2 = (c * 64) % 128
                        rtm = reads_tm[:, ti2 * 1024:(ti2 + 1) * 1024]
                        V.tensor_copy(rtm[tof2:tof2 + 64, (2 * j) * 64:(2 * j + 1) * 64],
                                      dump2[0:64, 0:64])
                        p.vbar(V.tensor_copy(rtm[tof2:tof2 + 64, (2 * j + 1) * 64:(2 * j + 2) * 64],
                                             dump2[64:128, 0:64]))
                        p.wait(TE, "V")
                        TE.matmul(psD[0:64, 0:1], onesf_row[0:1, 0:64],
                                  gl16[0:1, 2 * j:2 * j + 1], start=True, stop=True)
                        tt = p.done("TE", TE.matmul(psD[64:128, 0:1], onesf_row[0:1, 0:64],
                                                    gl16[0:1, 2 * j + 1:2 * j + 2],
                                                    start=True, stop=True))
                        p.wait(V, "TE", tt)
                        p.vbar(V.tensor_scalar(scident[:, :], ident[:, :], psD[:, 0:1],
                                               None, OP.mult))
                        p.vbar(V.reciprocal(stg[:, 0:1], batchG[:, j:j + 1]))
                        p.vbar(V.tensor_tensor(stg[:, 1:2], stg[:, 0:1], psD[:, 0:1],
                                               OP.mult))
                        p.vbar(V.tensor_scalar(ktm[:, :], ktm[:, :], stg[:, 1:2],
                                               None, OP.mult))
                        p.wait(TE, "V")
                        TE.matmul(psB[0:64, 0:64], scident[0:64, 0:64],
                                  s0p[0:64, j * 64:(j + 1) * 64], start=True, stop=False)
                        TE.matmul(psB[0:64, 0:64], ktm[0:64, :], xsol[0:64, :],
                                  start=False, stop=True)
                        TE.matmul(psB[64:128, 0:64], scident[64:128, 64:128],
                                  s0p[64:128, j * 64:(j + 1) * 64], start=True, stop=False)
                        tt = p.done("TE", TE.matmul(psB[64:128, 0:64], ktm[64:128, :],
                                                    xsol[64:128, :], start=False, stop=True))
                        p.wait(V, "TE", tt)
                        p.vbar(V.tensor_copy(s0p[:, j * 64:(j + 1) * 64], psB[:, 0:64]))
                p.vbar(V.tensor_copy(stg16[0:16, 0:2], scum16[:, TS - 2:TS]))
                p.vbar(V.tensor_copy(scprev[0:16, 0:1], stg16[0:16, 1:2]))
                if dbg:
                    for nm in list(dbg_h):
                        if nm.startswith(f"b{s}_"):
                            buf = {"reads": reads_tm, "vecs": vecs, "scum": scum16,
                                   "spn": spn16, "ab": ab32, "upd": updb, "cnd": cndb,
                                   "ca": cab, "sa": sab_, "qq": qbuf, "kk": kbuf,
                                   "vv": vbuf, "s0": s0p, "og": ogb,
                                   "ment": ment}[nm.split("_")[1]]
                            wid = dbg[nm]
                            nparts = {"scum": 16, "spn": 16}.get(nm.split("_")[1], 128)
                            p.wait(V, "TE")
                            p.wait(V, "S")
                            vt = p.done("V", V.tensor_copy(wstage[0:nparts, 0:wid],
                                                           buf[0:nparts, 0:wid]))
                            p.wait(Q, "V", vt)
                            td = p.done("Qd", Q.dma_start(dbg_h[nm], wstage[:, 0:wid]))
                            Q.wait_ge(p.sem["Qd"], td)
                            p.wait(V, "Qd")
                for ti2 in range(2):
                    rtm = reads_tm[:, ti2 * 1024:(ti2 + 1) * 1024]
                    p.vbar(V.tensor_tensor(sqtile[:, :], rtm[:, :], rtm[:, :], OP.mult))
                    p.vbar(V.tensor_reduce(hsum[:, 0:16],
                                           sqtile.rearrange("p (h d) -> p h d", h=16),
                                           mybir.AxisListType.X, OP.add))
                    p.vbar(V.tensor_scalar(hsum[:, 0:16], hsum[:, 0:16], 1.0 / 64, 1e-6,
                                           OP.mult, OP.add))
                    vt = p.tick["V"]
                    p.wait(S, "V", vt)
                    p.sbar(S.activation(hsum[:, 0:16], hsum[:, 0:16], AF.Sqrt))
                    p.wait(V, "S")
                    p.vbar(V.reciprocal(hsum[:, 0:16], hsum[:, 0:16]))
                    p.vbar(V.tensor_tensor(
                        rtm.rearrange("p (h d) -> p h d", h=16),
                        rtm.rearrange("p (h d) -> p h d", h=16),
                        hsum.rearrange("p (h o) -> p h o", o=1).broadcast_to([128, 16, 64]),
                        OP.mult))
                for ti2 in range(2):
                    for cc in range(8):
                        p.wait(TE, "V")
                        tt = p.done("TE", TE.transpose(
                            psE[:, 0:128],
                            reads_tm[:, ti2 * 1024 + cc * 128:ti2 * 1024 + (cc + 1) * 128],
                            ident[:, :]))
                        p.wait(V, "TE", tt)
                        p.done("V", V.tensor_copy(
                            mvfm[:, cc * TS + ti2 * 128:cc * TS + (ti2 + 1) * 128],
                            psE[:, 0:128]))
                p.wait(V, "V")
                for cc in range(8):
                    p.done("V", V.tensor_scalar(mvfm[:, cc * TS:(cc + 1) * TS],
                                                mvfm[:, cc * TS:(cc + 1) * TS],
                                                rmsm_col[:, cc:cc + 1], None, OP.mult))
                tt = p.tick["TE"]
                for cc in range(8):
                    p.wait(S, "V")
                    p.wait(S, "TE", tt)
                    stk = p.done("S", S.activation(x2b[:, :], vecs[:, cc * TS:(cc + 1) * TS],
                                                   AF.Square))
                    p.wait(TE, "S", stk)
                    tt = p.done("TE", TE.matmul(psC[0:1, 0:TS], onescol_b[:, 0:1], x2b[:, :],
                                                start=(cc == 0), stop=(cc == 7)))
                p.wait(V, "TE", tt)
                p.vbar(V.tensor_scalar(strow[0:1, 0:TS], psC[0:1, 0:TS], 1.0 / D, 1e-6,
                                       OP.mult, OP.add))
                vt = p.tick["V"]
                p.wait(S, "V", vt)
                p.sbar(S.activation(strow[0:1, 0:TS], strow[0:1, 0:TS], AF.Sqrt))
                p.wait(V, "S")
                p.vbar(V.reciprocal(strow[0:1, 0:TS], strow[0:1, 0:TS]))
                vt = p.done("V", V.tensor_copy(stat_b[0:1, 0:TS], strow[0:1, 0:TS]))
                p.wait(TE, "V", vt)
                tt = p.done("TE", TE.matmul(psC[:, 0:TS], onesrow_b[0:1, :],
                                            stat_b[0:1, 0:TS], start=True, stop=True))
                p.wait(V, "TE", tt)
                for cc in range(8):
                    p.vbar(V.tensor_tensor(mvfm[:, (8 + cc) * TS:(9 + cc) * TS],
                                           vecs[:, cc * TS:(cc + 1) * TS], psC[:, 0:TS],
                                           OP.mult))
                    p.done("V", V.tensor_scalar(mvfm[:, (8 + cc) * TS:(9 + cc) * TS],
                                                mvfm[:, (8 + cc) * TS:(9 + cc) * TS],
                                                rmsv_col[:, cc:cc + 1], None, OP.mult))
                for m in range(8):
                    for h in range(2):
                        b = 2 * m + h
                        if b == 0:
                            sw2_pre(dr["W_outb_s"], 0)
                        w = sw2_consume()
                        if b + 1 < 16:
                            sw2_pre(dr["W_outb_s"], b + 1)
                        for k8 in range(8):
                            k = h * 8 + k8
                            tt = p.done("TE", TE.matmul(psB[:, 0:TS],
                                                        w[:, k8 * 128:(k8 + 1) * 128],
                                                        mvfm[:, k * TS:(k + 1) * TS],
                                                        start=(k == 0), stop=(k == 15)))
                    p.wait(S, "TE", tt)
                    stk = p.done("S", S.activation(gatef[:, :], ogb[:, m * TS:(m + 1) * TS],
                                                   AF.Sigmoid))
                    p.wait(V, "S", stk)
                    p.wait(V, "TE", tt)
                    p.vbar(V.tensor_scalar(dump2[:, :], psB[:, 0:TS], boutb_col[:, m:m + 1],
                                           None, OP.add))
                    p.vbar(V.tensor_tensor(dump2[:, :], dump2[:, :], gatef[:, :], OP.mult))
                    vt = p.done("V", V.tensor_tensor(stream_c(m, sl), stream_c(m, sl),
                                                     dump2[:, :], OP.add))
                    p.wait(TE, "V", vt)

        if n_phases >= 4:
            p.arena_reset()
            e0b2 = p.aB(TS)
            e1b2 = p.aB(TS)
            d2b = p.aF(TS)
            d3b = p.aF(TS)
            ffn_phase(e0b2, e1b2, d2b, d3b, rowbuf, crow, browbuf01, colbuf)

        out_tm = x_tm_f
        for i in range(T // 128):
            p.wait(V, "Qd")
            p.wait(TE, "V")
            for c in range(8):
                p.wait(TE, "V")
                tt = p.done("TE", TE.transpose(psE[:, 0:128],
                                               stream_c(c)[:, i * 128:(i + 1) * 128], ident[:, :]))
                p.wait(V, "TE", tt)
                p.done("V", V.tensor_copy(out_tm[:, c * 128:(c + 1) * 128],
                                          psE[:, 0:128]))
            p.wait(Q, "V")
            td = p.done("Qd", Q.dma_start(out_h[i * 128:(i + 1) * 128, :], out_tm[:, :]))
            Q.wait_ge(p.sem["Qd"], td)
        Q.wait_ge(p.sem["Qd"], p.tick["Qd"])

    return nc


def kernel(**inputs):
    pre = _host_prepare(inputs)
    nc = build_program(DEBUG_PHASES)
    x = np.asarray(inputs["L0_attn_a_x"], np.float32)
    shared = {k: np.asarray(v) for k, v in pre.items()}
    in_maps = []
    for b in range(8):
        m = dict(shared)
        m["x_in"] = np.ascontiguousarray(x[b])
        in_maps.append(m)
    res = run_bass_kernel_spmd(nc, in_maps, core_ids=list(range(8)))
    return np.stack([np.asarray(r["out"], np.float32) for r in res.results], axis=0)


# version 07 (final)
# speedup vs baseline: 1.0939x; 1.0939x over previous
import contextlib
import numpy as np
import ml_dtypes
import concourse.bass as bass
import concourse.mybir as mybir
from concourse.bass_utils import run_bass_kernel_spmd

F32 = mybir.dt.float32
BF16 = mybir.dt.bfloat16
AF = mybir.ActivationFunctionType
OP = mybir.AluOpType
BFNP = ml_dtypes.bfloat16

T = 2048
D = 1024
DI = 2048
NSTATE = 16
TS = 256
NSL = T // TS
NG = DI // 8
CTRL = 6688
CTRLP = 6784
H = 16
HD = 64
L = 64
NPAIR = D // 2
MAGIC = 12582912.0

DEBUG_PHASES = 99


def _host_prepare(inputs):
    I = inputs
    f32 = np.float32
    asc = np.ascontiguousarray
    pre = {}
    pre["W_inT"] = asc(I["L0_attn_a_W_in"].astype(f32).T)
    cw = I["L0_attn_a_conv_w"].astype(f32)[:, 0, :]
    pre["conv_w_l"] = asc(cw.reshape(16, 128, 4).transpose(1, 0, 2).reshape(128, 64))
    pre["conv_b_l"] = asc(I["L0_attn_a_conv_b"].astype(f32).reshape(16, 128).T)
    wxt = I["L0_attn_a_W_x"].astype(f32).T
    wxp = np.zeros((DI, 128), f32)
    wxp[:, 0:16] = wxt[:, 64:80]
    wxp[:, 32:48] = wxt[:, 80:96]
    wxp[:, 64:128] = wxt[:, 0:64]
    pre["W_xT"] = asc(wxp)
    pre["W_dtT"] = asc(I["L0_attn_a_W_dt"].astype(f32).T)
    pre["b_dt_l"] = asc(I["L0_attn_a_b_dt"].astype(f32).reshape(16, 128).T)
    pre["A_log_g"] = asc(I["L0_attn_a_A_log"].astype(f32).reshape(NG, 128))
    pre["D_param_l"] = asc(I["L0_attn_a_D_param"].astype(f32).reshape(16, 128).T)
    pre["W_outaT"] = asc(I["L0_attn_a_W_out"].astype(f32).T)
    pre["gate_w"] = asc(I["L0_ffn_gate_w"].astype(f32))
    pre["w0T"] = asc(I["L0_ffn_w0"].astype(f32).T)
    pre["b0_l"] = asc(I["L0_ffn_b0"].astype(f32).reshape(8, 128).T)
    pre["w1T"] = asc(I["L0_ffn_w1"].astype(f32).T)
    pre["b1_l"] = asc(I["L0_ffn_b1"].astype(f32).reshape(8, 128).T)
    SIZES = [D, D // 2, D, D, D, D, H, H, D]
    OFF = np.cumsum([0] + SIZES)
    perm = np.concatenate([
        np.arange(OFF[0], OFF[1])[0::2], np.arange(OFF[0], OFF[1])[1::2],
        np.arange(OFF[1], OFF[2]),
        np.arange(OFF[2], OFF[3])[0::2], np.arange(OFF[2], OFF[3])[1::2],
        np.arange(OFF[3], OFF[4]), np.arange(OFF[4], OFF[5]),
        np.arange(OFF[5], OFF[6]), np.arange(OFF[6], OFF[7]),
        np.arange(OFF[7], OFF[8]), np.arange(OFF[8], OFF[9])])
    wc = I["L0_attn_b_W_ctrl"].astype(f32)[:, perm]
    bc = I["L0_attn_b_b_ctrl"].astype(f32)[perm]
    wcp = np.zeros((D, CTRLP), f32)
    bcp = np.zeros(CTRLP, f32)
    wcp[:, 0:5632] = wc[:, 0:5632]
    bcp[0:5632] = bc[0:5632]
    wcp[:, 5632:5648] = wc[:, 5632:5648]
    bcp[5632:5648] = bc[5632:5648]
    wcp[:, 5696:5712] = wc[:, 5648:5664]
    bcp[5696:5712] = bc[5648:5664]
    wcp[:, 5760:6784] = wc[:, 5664:6688]
    bcp[5760:6784] = bc[5664:6688]
    pre["W_ctrlT_p"] = asc(wcp)
    pre["b_ctrl_cols"] = asc(bcp.reshape(53, 128).T)
    ps128 = np.zeros((2, 128), f32)
    ps128[0, 0:64] = 1; ps128[1, 64:128] = 1
    pre["PAIRSEL128F"] = ps128
    mA = np.where(np.tril(np.ones((L, L)), -1) > 0, 0.0, -1e30).astype(f32)
    mG = np.where(np.triu(np.ones((L, L)), 0) > 0, 0.0, -1e30).astype(f32)
    pre["MBIAS_A"] = asc(np.concatenate([mA, mA], 0))
    pre["MBIAS_G"] = asc(np.concatenate([mG, mG], 0))
    pre["W_block_l"] = asc(I["L0_attn_b_W_block"].astype(f32).reshape(NPAIR, 4)
                           .reshape(4, 128, 4).transpose(1, 0, 2).reshape(128, 16))
    for nm, key in (("qconv", "L0_attn_b_qconv_w"), ("kconv", "L0_attn_b_kconv_w"),
                    ("vconv", "L0_attn_b_vconv_w")):
        c = I[key].astype(f32)
        pre[nm + "_l"] = asc(c.reshape(8, 128, 4).transpose(1, 0, 2).reshape(128, 32))
    pre["rms_m_l"] = asc(np.tile(I["L0_attn_b_rms_m_w"].astype(f32), 16)
                         .reshape(8, 128).T)
    pre["rms_v_perm"] = asc(np.concatenate([I["L0_attn_b_rms_v_w"].astype(f32)[0::2],
                                            I["L0_attn_b_rms_v_w"].astype(f32)[1::2]])
                            .reshape(8, 128).T)
    Wo = I["L0_attn_b_W_out"].astype(f32)
    rperm = np.concatenate([np.arange(0, D), D + np.arange(0, D)[0::2],
                            D + np.arange(0, D)[1::2]])
    pre["W_outb_p"] = asc(Wo[rperm, :])
    pre["b_outb_l"] = asc(I["L0_attn_b_b_out"].astype(f32).reshape(8, 128).T)

    def kb(W):
        KT, MT = W.shape
        A = W.reshape(KT // 1024, 8, 128, MT // 128, 128)
        return asc(A.transpose(3, 0, 2, 1, 4).reshape((MT // 128) * (KT // 1024) * 128,
                                                      1024))
    pre["W_in_s"] = kb(pre["W_inT"])
    pre["W_outa_s"] = kb(pre["W_outaT"])
    pre["w0_s"] = kb(pre["w0T"])
    pre["w1_s"] = kb(pre["w1T"])
    pre["W_ctrl_s"] = kb(pre["W_ctrlT_p"])
    pre["W_outb_s"] = kb(pre["W_outb_p"])
    gw = pre["gate_w"]
    pre["gate_s"] = asc(gw.reshape(8, 128, 8).transpose(1, 0, 2).reshape(128, 64))
    pre["identity"] = np.eye(128, dtype=BFNP)
    pre["identity_f32"] = np.eye(128, dtype=f32)
    s64u = np.zeros((8, 64, 128), BFNP)
    negm = np.zeros((8, 64, 128), BFNP)
    sy64 = np.zeros((8, 128, 64), BFNP)
    for lp in range(8):
        for dd in range(8):
            k = lp * 8 + dd
            s64u[lp, k, dd * NSTATE:(dd + 1) * NSTATE] = 1
            negm[lp, k, dd * NSTATE:(dd + 1) * NSTATE] = -1
            sy64[lp, dd * NSTATE:(dd + 1) * NSTATE, k] = 1
    pre["SEL64U"] = asc(s64u.reshape(8 * 64, 128))
    pre["NEGM64"] = asc(negm.reshape(8 * 64, 128))
    pre["SEL64Y"] = asc(sy64.transpose(1, 0, 2).reshape(128, 8 * 64))
    s16 = np.zeros((16, 128), BFNP)
    for n in range(16):
        s16[n, n::NSTATE] = 1
    pre["SEL16"] = s16
    sq = np.zeros((8 * 128, 16), BFNP)
    sqe = np.zeros((16, 8 * 128), BFNP)
    for j in range(8):
        sq[j * 128 + 0:j * 128 + 64, 2 * j] = 1
        sq[j * 128 + 64:j * 128 + 128, 2 * j + 1] = 1
        sqe[2 * j, j * 128 + 0:j * 128 + 64] = 1
        sqe[2 * j + 1, j * 128 + 64:j * 128 + 128] = 1
    pre["SELQ"] = sq
    pre["SELQE"] = sqe
    psel = np.zeros((2, 256), BFNP)
    psel[0, 0:128] = 1
    psel[1, 128:256] = 1
    pre["PAIRSEL"] = psel
    tl = np.tril(np.ones((L, L), f32), -1)
    ti = np.tril(np.ones((L, L), f32), 0)
    pre["MASK_SL"] = np.concatenate([tl, tl], 0)
    pre["MASK_IL"] = np.concatenate([ti, ti], 0)
    return pre


class P:
    def __init__(self, nc, stack):
        self.nc = nc
        self.st = stack
        self.V = nc.vector
        self.S = nc.scalar
        self.TE = nc.tensor
        self.G = nc.gpsimd
        self.Q = nc.sync
        self.sem = {}
        self.tick = {}
        for name in ("V", "S", "TE", "G", "Qd", "Qw", "Qg"):
            self.sem[name] = stack.enter_context(nc.semaphore("sem_" + name))
            self.tick[name] = 0

    def done(self, name, ins):
        ins.then_inc(self.sem[name], 16 if name.startswith("Q") else 1)
        self.tick[name] += 16 if name.startswith("Q") else 1
        return self.tick[name]

    def wait(self, eng, name, tick=None):
        t = self.tick[name] if tick is None else tick
        if name.startswith("Q"):
            qeng = self.G if name == "Qg" else self.Q
            if eng is not qeng:
                qeng.wait_ge(self.sem[name], t)
        eng.wait_ge(self.sem[name], t)

    def vbar(self, ins):
        self.done("V", ins)
        self.V.wait_ge(self.sem["V"], self.tick["V"])

    def sbar(self, ins):
        self.done("S", ins)
        self.S.wait_ge(self.sem["S"], self.tick["S"])

    def sb(self, name, shape, dtype):
        return self.st.enter_context(self.nc.sbuf_tensor(name, shape, dtype))

    def arena_init(self, nbytes_b, nbytes_f):
        self._arB = self.sb("arenaB", [128, nbytes_b // 2], BF16)
        self._arF = self.sb("arenaF", [128, nbytes_f // 4], F32)
        self._offB = 0
        self._offF = 0

    def arena_reset(self):
        self._offB = 0
        self._offF = 0

    def aB(self, n):
        v = self._arB[:, self._offB:self._offB + n]
        self._offB += n
        assert self._offB <= self._arB.shape[1], "arenaB overflow"
        return v

    def aF(self, n):
        v = self._arF[:, self._offF:self._offF + n]
        self._offF += n
        assert self._offF <= self._arF.shape[1], "arenaF overflow"
        return v

    def psum(self, name, shape, dtype=F32):
        return self.st.enter_context(self.nc.psum_tensor(name, shape, dtype))


def build_program(n_phases=99, dbg=None):
    nc = bass.Bass("TRN2", target_bir_lowering=False)
    dr = {}

    def din(name, shape, dtype=F32):
        dr[name] = nc.dram_tensor(name, list(shape), dtype, kind="ExternalInput").ap()

    din("x_in", (T, D))
    din("W_inT", (D, 4096)); din("conv_w_l", (128, 64)); din("conv_b_l", (128, 16))
    din("W_xT", (DI, 128)); din("W_dtT", (64, DI)); din("b_dt_l", (128, 16))
    din("A_log_g", (NG, 128)); din("D_param_l", (128, 16)); din("W_outaT", (DI, D))
    din("gate_w", (D, 8)); din("w0T", (D, D)); din("b0_l", (128, 8))
    din("w1T", (D, D)); din("b1_l", (128, 8))
    din("W_ctrlT_p", (D, CTRLP)); din("b_ctrl_cols", (128, 53)); din("W_block_l", (128, 16))
    din("PAIRSEL128F", (2, 128)); din("MBIAS_A", (128, L)); din("MBIAS_G", (128, L))
    din("qconv_l", (128, 32)); din("kconv_l", (128, 32)); din("vconv_l", (128, 32))
    din("rms_m_l", (128, 8)); din("rms_v_perm", (128, 8))
    din("W_outb_p", (2 * D, D)); din("b_outb_l", (128, 8))
    din("W_in_s", (4096, 1024)); din("W_outa_s", (2048, 1024))
    din("w0_s", (1024, 1024)); din("w1_s", (1024, 1024))
    din("W_ctrl_s", (6784, 1024)); din("W_outb_s", (2048, 1024))
    din("gate_s", (128, 64))
    din("identity", (128, 128), BF16); din("identity_f32", (128, 128))
    din("SEL64U", (512, 128), BF16); din("NEGM64", (512, 128), BF16); din("SEL64Y", (128, 512), BF16)
    din("SEL16", (16, 128), BF16); din("SELQ", (8 * 128, 16), BF16)
    din("SELQE", (16, 8 * 128), BF16)
    din("MASK_SL", (128, L)); din("MASK_IL", (128, L)); din("PAIRSEL", (2, 256), BF16)
    out_h = nc.dram_tensor("out", [T, D], F32, kind="ExternalOutput").ap()
    dbg_h = {}
    if dbg:
        for nm, width in dbg.items():
            dbg_h[nm] = nc.dram_tensor("dbg_" + nm, [128, width], F32, kind="ExternalOutput").ap()

    with contextlib.ExitStack() as st:
        p = P(nc, st)
        V, S, TE, G, Q = p.V, p.S, p.TE, p.G, p.Q

        stream = p.sb("stream", [128, 8 * T], BF16)

        def stream_c(c, sl=slice(None)):
            base = c * T
            if sl == slice(None):
                return stream[:, base:base + T]
            return stream[:, base + sl.start:base + sl.stop]

        ident = p.sb("ident", [128, 128], BF16)
        identf = p.sb("identf", [128, 128], F32)
        onescol_b = p.sb("onescol_b", [128, 1], BF16)
        onesrow_b = p.sb("onesrow_b", [1, 128], BF16)
        ones64_b = p.sb("ones64_b", [1, 64], BF16)
        sel64u = p.sb("sel64u", [128, 8 * 128], BF16)
        negm64 = p.sb("negm64", [64, 8 * 128], BF16)
        sel64y = p.sb("sel64y", [128, 8 * 64], BF16)
        sel16 = p.sb("sel16", [16, 128], BF16)
        selq = p.sb("selq", [128, 8 * 16], BF16)
        selq8 = p.sb("selq8", [128, 128], BF16)
        selqe = p.sb("selqe", [16, 8 * 128], BF16)
        pairsel = p.sb("pairsel", [2, 256], BF16)
        mask_sl = p.sb("mask_sl", [128, L], F32)
        mask_il = p.sb("mask_il", [128, L], F32)
        ccol = {}
        for cn, cv in [("one", 1.0), ("eps_ln", 1e-5), ("eps_rms", 1e-6),
                       ("eps_q", 1e-12), ("zero", 0.0), ("tiny", 1e-12)]:
            ccol[cn] = p.sb("c_" + cn, [128, 1], F32)
            V.memset(ccol[cn][:, :], cv)
        V.memset(onescol_b[:, :], 1.0)
        V.memset(onesrow_b[:, :], 1.0)
        V.memset(ones64_b[:, :], 1.0)

        psA = p.psum("psA", [128, 2048], F32)
        psB = p.psum("psB", [128, 512], F32)
        psC = p.psum("psC", [128, 512], F32)
        psD = p.psum("psD", [128, 512], F32)
        psE = p.psum("psE", [128, 1024], BF16)
        p.arena_init(2 * 52032, 4 * 2560)

        for tgt, src in [(ident, dr["identity"]), (identf, dr["identity_f32"]),
                         (sel16, dr["SEL16"]),
                         (selqe, dr["SELQE"]), (pairsel, dr["PAIRSEL"]),
                         (mask_sl, dr["MASK_SL"]),
                         (mask_il, dr["MASK_IL"])]:
            p.done("Qd", Q.dma_start(tgt[:, :], src))
        for v in range(8):
            p.done("Qd", Q.dma_start(sel64u[0:64, v * 128:(v + 1) * 128],
                                     dr["SEL64U"][v * 64:(v + 1) * 64, :]))
            p.done("Qd", Q.dma_start(sel64u[64:128, v * 128:(v + 1) * 128],
                                     dr["SEL64U"][v * 64:(v + 1) * 64, :]))
            p.done("Qd", Q.dma_start(negm64[:, v * 128:(v + 1) * 128],
                                     dr["NEGM64"][v * 64:(v + 1) * 64, :]))
        p.done("Qd", Q.dma_start(sel64y[:, :], dr["SEL64Y"]))
        for j in range(8):
            p.done("Qd", Q.dma_start(selq8[:, j * 16:(j + 1) * 16],
                                     dr["SELQ"][j * 128:(j + 1) * 128, :]))

        x_tm_f = p.sb("x_tm_f", [128, D], F32)
        x_tm_b = p.sb("x_tm_b", [128, D], BF16)
        for i in range(T // 128):
            td = p.done("Qd", Q.dma_start(x_tm_f[:, :], dr["x_in"][i * 128:(i + 1) * 128, :]))
            p.wait(V, "Qd", td)
            vt = p.done("V", V.tensor_copy(x_tm_b[:, :], x_tm_f[:, :]))
            p.wait(TE, "V", vt)
            for c in range(8):
                p.wait(TE, "V")
                tt = p.done("TE", TE.transpose(psE[:, 0:128],
                                               x_tm_b[:, c * 128:(c + 1) * 128], ident[:, :]))
                p.wait(V, "TE", tt)
                vt2 = p.done("V", V.tensor_copy(stream_c(c)[:, i * 128:(i + 1) * 128],
                                                psE[:, 0:128]))
            p.wait(Q, "V", vt2)

        wstage = p.sb("wstage", [128, 1024], F32)
        b0_col = p.sb("b0_col", [128, 8], F32)
        b1_col = p.sb("b1_col", [128, 8], F32)
        p.done("Qd", Q.dma_start(b0_col[:, :], dr["b0_l"]))
        p.done("Qd", Q.dma_start(b1_col[:, :], dr["b1_l"]))
        wtile = [p.sb(f"wtile{i}", [128, 1024], BF16) for i in range(2)]

        wstage2 = p.sb("wstage2", [128, 1024], F32)
        wst2 = [wstage, wstage2]
        sw2_n = [0, 0]
        sw2_td = [0, 0]
        sw2_kw = [1024, 1024]
        sw2_cast = [0, 0]
        sw2_prev_te = [0]

        sw2_qe = [Q, G]
        sw2_qn = ["Qw", "Qg"]

        def sw2_pre(dram_ap, b, kw=1024):
            which = sw2_n[0] % 2
            sw2_n[0] += 1
            p.wait(sw2_qe[which], "S", sw2_cast[which])
            sw2_td[which] = p.done(sw2_qn[which], sw2_qe[which].dma_start(
                wst2[which][:, 0:kw], dram_ap[b * 128:(b + 1) * 128, 0:kw]))
            sw2_kw[which] = kw

        def sw2_consume():
            which = sw2_n[1] % 2
            sw2_n[1] += 1
            kw = sw2_kw[which]
            cur_te = p.tick["TE"]
            p.wait(S, sw2_qn[which], sw2_td[which])
            p.wait(S, "TE", sw2_prev_te[0])
            sw2_prev_te[0] = cur_te
            sw2_cast[which] = p.done("S", S.activation(
                wtile[which][:, 0:kw], wst2[which][:, 0:kw], AF.Copy))
            p.wait(TE, "S", sw2_cast[which])
            p.wait(TE, "V")
            return wtile[which]

        xn = p.sb("xn", [128, 8 * TS], BF16)

        def xn_c(c):
            return xn[:, c * TS:(c + 1) * TS]

        x2b = p.sb("x2b", [128, TS], BF16)
        strow = p.sb("strow", [1, 3 * TS], F32)
        stat_b = p.sb("stat_b", [1, 2 * TS], BF16)

        def ln_slice(s, dst_c):
            sl = slice(s * TS, (s + 1) * TS)
            p.wait(TE, "V")
            for c in range(8):
                tt = p.done("TE", TE.matmul(psA[0:1, 0:TS], onescol_b[:, 0:1],
                                            stream_c(c, sl), start=(c == 0), stop=(c == 7)))
            for c in range(8):
                p.wait(S, "TE")
                stk = p.done("S", S.activation(x2b[:, :], stream_c(c, sl), AF.Square))
                p.wait(TE, "S", stk)
                tt = p.done("TE", TE.matmul(psA[0:1, 512:512 + TS], onescol_b[:, 0:1],
                                            x2b[:, :], start=(c == 0), stop=(c == 7)))
            p.wait(V, "TE", tt)
            p.vbar(V.tensor_scalar(strow[0:1, 0:TS], psA[0:1, 0:TS], 1.0 / D, None, OP.mult))
            p.vbar(V.tensor_scalar(strow[0:1, TS:2 * TS], psA[0:1, 512:512 + TS],
                                   1.0 / D, None, OP.mult))
            p.vbar(V.tensor_tensor(strow[0:1, 2 * TS:3 * TS], strow[0:1, 0:TS],
                                   strow[0:1, 0:TS], OP.mult))
            vt = p.done("V", V.tensor_tensor(strow[0:1, TS:2 * TS], strow[0:1, TS:2 * TS],
                                             strow[0:1, 2 * TS:3 * TS], OP.subtract))
            p.wait(S, "V", vt)
            stk = p.done("S", S.activation(strow[0:1, 2 * TS:3 * TS], strow[0:1, TS:2 * TS],
                                           AF.Sqrt, bias=ccol["eps_ln"][0:1, 0:1]))
            p.wait(V, "S", stk)
            p.vbar(V.reciprocal(strow[0:1, TS:2 * TS], strow[0:1, 2 * TS:3 * TS]))
            p.vbar(V.tensor_tensor(strow[0:1, 2 * TS:3 * TS], strow[0:1, 0:TS],
                                   strow[0:1, TS:2 * TS], OP.mult))
            p.vbar(V.tensor_scalar(strow[0:1, 2 * TS:3 * TS], strow[0:1, 2 * TS:3 * TS],
                                   -1.0, None, OP.mult))
            V.tensor_copy(stat_b[0:1, 0:TS], strow[0:1, TS:2 * TS])
            vt = p.done("V", V.tensor_copy(stat_b[0:1, TS:2 * TS], strow[0:1, 2 * TS:3 * TS]))
            p.wait(TE, "V", vt)
            p.done("TE", TE.matmul(psA[:, 1024:1024 + TS], onesrow_b[0:1, :],
                                   stat_b[0:1, 0:TS], start=True, stop=True))
            tt = p.done("TE", TE.matmul(psA[:, 1536:1536 + TS], onesrow_b[0:1, :],
                                        stat_b[0:1, TS:2 * TS], start=True, stop=True))
            p.wait(V, "TE", tt)
            for c in range(8):
                p.vbar(V.tensor_tensor(dst_c(c), stream_c(c, sl), psA[:, 1024:1024 + TS], OP.mult))
                p.done("V", V.tensor_tensor(dst_c(c), dst_c(c), psA[:, 1536:1536 + TS], OP.add))

        if n_phases >= 1:
            p.arena_reset()
            xpath = p.aB(16 * (TS + 3))
            zsilu = p.aB(16 * TS)
            xcb = p.aB(16 * TS)
            delta = p.aB(16 * TS)
            ub = p.aB(16 * TS)
            ymb = p.aB(16 * TS)
            a_exp = p.aB(8 * TS)
            bu_t = p.aB(TS)
            dump = p.aB(TS)
            pscan = p.aB(TS)
            sp32 = p.aF(TS)
            gcum = p.aF(TS)
            invg_t = p.aF(TS)
            dhi_t = p.aF(TS)
            zcor = p.aF(TS)

            def xpath_c(c):
                return xpath[:, c * (TS + 3):(c + 1) * (TS + 3)]

            def chunk(buf, c):
                return buf[:, c * TS:(c + 1) * TS]

            dtlow_b = p.sb("dtlow_b", [64, TS], BF16)
            b_rows = p.sb("b_rows", [16, TS], BF16)
            c_rows = p.sb("c_rows", [16, TS], BF16)
            brep = p.sb("brep", [128, TS], BF16)
            crep = p.sb("crep", [128, TS], BF16)
            convdiag = p.aB(64 * 128)
            etiles = p.aB((NG // 2) * 128)
            arow = x_tm_f[:, 256:512]
            arow_b = p.sb("arow_b", [128, 128], BF16)
            pstate = p.sb("pstate", [128, NG], F32)
            zstate = p.sb("zstate", [128, NG], F32)
            zstate2 = x_tm_f[:, 0:NG]
            pst_stage = p.sb("pst_stage", [128, 8], F32)
            convb_col = p.sb("convb_col", [128, 16], F32)
            bdt_col = p.sb("bdt_col", [128, 16], F32)
            dpar_col = p.sb("dpar_col", [128, 16], F32)
            wx_b = p.sb("wx_b", [128, 16 * 128], BF16)
            wdt_b = p.sb("wdt_b", [64, DI], BF16)

            V.memset(pstate[:, :], 0.0)
            V.memset(zstate[:, :], 0.0)
            V.memset(zstate2[:, :], 0.0)
            for c in range(16):
                V.memset(xpath_c(c)[:, 0:3], 0.0)

            for half in range(2):
                p.wait(Q, "TE")
                td = p.done("Qd", Q.dma_start(arow[:, 0:128],
                                              dr["A_log_g"][half * 128:(half + 1) * 128, :]))
                p.wait(S, "Qd", td)
                p.wait(S, "TE")
                stk = p.done("S", S.activation(arow_b[:, 0:128], arow[:, 0:128], AF.Exp))
                p.wait(TE, "S", stk)
                p.wait(V, "S", stk)
                for gi in range(128):
                    g = half * 128 + gi
                    lp = g % 8
                    tt = p.done("TE", TE.matmul(psD[0:64, 0:128],
                                                ident[:, gi:gi + 1].broadcast_to([128, 64]),
                                                arow_b[:, 0:128],
                                                start=True, stop=True))
                    p.wait(V, "TE", tt)
                    poff = ((g % 16) // 8) * 64
                    slot = (g // 16) * 8 + (g % 8)
                    vt = p.done("V", V.tensor_tensor(
                        etiles[poff:poff + 64, slot * 128:(slot + 1) * 128],
                        psD[0:64, 0:128], negm64[:, lp * 128:(lp + 1) * 128], OP.mult))
                    p.wait(TE, "V", vt)

            convw_col = x_tm_f[:, 512:576]
            td = p.done("Qd", Q.dma_start(convw_col[:, :], dr["conv_w_l"]))
            p.wait(V, "Qd")
            for c in range(16):
                for j in range(4):
                    p.done("V", V.tensor_scalar(convdiag[:, (c * 4 + j) * 128:(c * 4 + j + 1) * 128],
                                                ident[:, :], convw_col[:, c * 4 + j:c * 4 + j + 1],
                                                None, OP.mult))
            for tgt, src in [(convb_col, dr["conv_b_l"]), (bdt_col, dr["b_dt_l"]),
                             (dpar_col, dr["D_param_l"])]:
                p.done("Qd", Q.dma_start(tgt[:, :], src))
            for c in range(16):
                td = p.done("Qd", Q.dma_start(wstage[:, 0:128], dr["W_xT"][c * 128:(c + 1) * 128, :]))
                p.wait(V, "Qd", td)
                vt = p.done("V", V.tensor_copy(wx_b[:, c * 128:(c + 1) * 128], wstage[:, 0:128]))
                p.wait(Q, "V", vt)
            for hh in range(2):
                td = p.done("Qd", Q.dma_start(wstage[0:64, 0:1024],
                                              dr["W_dtT"][:, hh * 1024:(hh + 1) * 1024]))
                p.wait(V, "Qd", td)
                vt = p.done("V", V.tensor_copy(wdt_b[:, hh * 1024:(hh + 1) * 1024],
                                               wstage[0:64, 0:1024]))
                p.wait(Q, "V", vt)

            for s in range(NSL):
                sl = slice(s * TS, (s + 1) * TS)
                ln_slice(s, xn_c)
                for m in range(32):
                    if m == 0:
                        sw2_pre(dr["W_in_s"], 0)
                    w = sw2_consume()
                    if m + 1 < 32:
                        sw2_pre(dr["W_in_s"], m + 1)
                    for k in range(8):
                        tt = p.done("TE", TE.matmul(psB[:, 0:TS], w[:, k * 128:(k + 1) * 128],
                                                    xn_c(k),
                                                    start=(k == 0), stop=(k == 7)))
                    if m < 16:
                        p.wait(V, "TE", tt)
                        p.done("V", V.tensor_copy(xpath_c(m)[:, 3:3 + TS], psB[:, 0:TS]))
                    else:
                        p.wait(S, "TE", tt)
                        stk = p.done("S", S.activation(chunk(zsilu, m - 16), psB[:, 0:TS], AF.Silu))
                        p.wait(TE, "S", stk)
                p.wait(TE, "V")
                for c in range(16):
                    for j in range(4):
                        off = 3 - j
                        tt = p.done("TE", TE.matmul(psB[:, 0:TS],
                                                    convdiag[:, (c * 4 + j) * 128:(c * 4 + j + 1) * 128],
                                                    xpath_c(c)[:, 3 - off:3 - off + TS],
                                                    start=(j == 0), stop=(j == 3)))
                    p.wait(S, "TE", tt)
                    stk = p.done("S", S.activation(chunk(xcb, c), psB[:, 0:TS], AF.Silu,
                                                   bias=convb_col[:, c:c + 1]))
                    p.wait(TE, "S", stk)
                p.wait(V, "TE")
                for c in range(16):
                    p.done("V", V.tensor_copy(xpath_c(c)[:, 0:3], xpath_c(c)[:, TS:TS + 3]))
                for k in range(16):
                    tt = p.done("TE", TE.matmul(psB[:, 0:TS], wx_b[:, k * 128:(k + 1) * 128],
                                                chunk(xcb, k), start=(k == 0), stop=(k == 15)))
                p.wait(V, "TE", tt)
                V.tensor_copy(dtlow_b[:, :], psB[64:128, 0:TS])
                V.tensor_copy(b_rows[:, :], psB[0:16, 0:TS])
                vt = p.done("V", V.tensor_copy(c_rows[:, :], psB[32:48, 0:TS]))
                p.wait(TE, "V", vt)
                tt = p.done("TE", TE.matmul(psB[:, 0:TS], sel16[:, :], b_rows[0:16, :],
                                            start=True, stop=True))
                tt2 = p.done("TE", TE.matmul(psC[:, 0:TS], sel16[:, :], c_rows[0:16, :],
                                             start=True, stop=True))
                p.wait(V, "TE", tt2)
                V.tensor_copy(brep[:, :], psB[:, 0:TS])
                vt = p.done("V", V.tensor_copy(crep[:, :], psC[:, 0:TS]))
                for m in range(16):
                    p.wait(TE, "V")
                    tt = p.done("TE", TE.matmul(psB[:, 0:TS], wdt_b[:, m * 128:(m + 1) * 128],
                                                dtlow_b[:, :], start=True, stop=True))
                    p.wait(S, "TE", tt)
                    p.sbar(S.activation(sp32[:, :], psB[:, 0:TS], AF.Exp,
                                        bias=bdt_col[:, m:m + 1]))
                    stk = p.done("S", S.activation(sp32[:, :], sp32[:, :], AF.Ln,
                                                   bias=ccol["one"][:, 0:1]))
                    p.wait(V, "S", stk)
                    vt = p.done("V", V.tensor_scalar(chunk(delta, m), sp32[:, :], 1e-4, 10.0,
                                                     OP.add, OP.min))
                    p.wait(TE, "V", vt)
                    p.wait(S, "V", vt)
                p.wait(V, "V")
                for c in range(16):
                    p.done("V", V.tensor_tensor(chunk(ub, c), chunk(delta, c), chunk(xcb, c),
                                                OP.mult))
                p.wait(TE, "V")
                for gh in range(32):
                    dci, hf = gh // 2, gh % 2
                    doff = hf * 64
                    g0 = gh * 8
                    for bat in range(2):
                        for gsub4 in range(4):
                            gsub = bat * 4 + gsub4
                            g = g0 + gsub
                            poff = ((g % 16) // 8) * 64
                            slot = (g // 16) * 8 + (g % 8)
                            tt = p.done("TE", TE.matmul(
                                psA[:, gsub4 * 512:gsub4 * 512 + TS],
                                etiles[poff:poff + 64, slot * 128:(slot + 1) * 128],
                                chunk(delta, dci)[doff:doff + 64, :], start=True, stop=True))
                        p.wait(S, "TE", tt)
                        stk = p.done("S", S.activation(
                            a_exp[:, bat * 4 * TS:(bat + 1) * 4 * TS],
                            psA.rearrange("p (b t) -> p b t", t=512)[:, :, 0:TS],
                            AF.Exp))
                        p.wait(TE, "S", stk)
                    p.wait(V, "S", stk)
                    for gsub in range(8):
                        g = g0 + gsub
                        tt = p.done("TE", TE.matmul(psB[:, 0:TS],
                                                    sel64u[doff:doff + 64, gsub * 128:(gsub + 1) * 128],
                                                    chunk(ub, dci)[doff:doff + 64, :],
                                                    start=True, stop=True))
                        p.wait(V, "TE", tt)
                        p.vbar(V.tensor_tensor(bu_t[:, :], psB[:, 0:TS], brep[:, :], OP.mult))
                        p.vbar(V.tensor_scalar(bu_t[:, :], bu_t[:, :], 10.0, -10.0,
                                               OP.min, OP.max))
                        p.vbar(V.tensor_tensor_scan(pscan[:, :],
                                                    a_exp[:, gsub * TS:(gsub + 1) * TS],
                                                    bu_t[:, :], pstate[:, g:g + 1],
                                                    OP.mult, OP.add))
                        p.vbar(V.tensor_copy(pst_stage[:, 0:2], pscan[:, TS - 2:TS]))
                        V.tensor_copy(pstate[:, g:g + 1], pst_stage[:, 1:2])
                        p.vbar(V.tensor_scalar(dhi_t[:, :], pscan[:, :], -1.0, 20.0,
                                               OP.mult, OP.add))
                        p.vbar(V.tensor_scalar(invg_t[:, :], dhi_t[:, :], 1.0, -40.0,
                                               OP.mult, OP.add))
                        p.vbar(V.tensor_tensor_scan(gcum[:, :],
                                                    a_exp[:, gsub * TS:(gsub + 1) * TS],
                                                    dhi_t[:, :], zstate[:, g:g + 1],
                                                    OP.mult, OP.min))
                        p.vbar(V.tensor_tensor_scan(zcor[:, :],
                                                    a_exp[:, gsub * TS:(gsub + 1) * TS],
                                                    invg_t[:, :], zstate2[:, g:g + 1],
                                                    OP.mult, OP.max))
                        V.tensor_copy(zstate[:, g:g + 1], gcum[:, TS - 1:TS])
                        p.vbar(V.tensor_copy(zstate2[:, g:g + 1], zcor[:, TS - 1:TS]))
                        p.vbar(V.tensor_tensor(gcum[:, :], gcum[:, :], zcor[:, :], OP.add))
                        p.vbar(V.tensor_tensor(pscan[:, :], pscan[:, :], gcum[:, :], OP.add))
                        vt = p.done("V", V.tensor_tensor(dump[:, :], pscan[:, :],
                                                         crep[:, :], OP.mult))
                        p.wait(TE, "V", vt)
                        tt = p.done("TE", TE.matmul(psC[0:64, 0:TS],
                                                    sel64y[:, gsub * 64:(gsub + 1) * 64],
                                                    dump[:, :],
                                                    start=(gsub == 0), stop=(gsub == 7)))
                    p.wait(V, "TE", tt)
                    p.done("V", V.tensor_copy(chunk(ymb, dci)[doff:doff + 64, :],
                                              psC[0:64, 0:TS]))
                for c in range(16):
                    p.vbar(V.scalar_tensor_tensor(sp32[:, :], chunk(xcb, c), dpar_col[:, c:c + 1],
                                                  chunk(ymb, c), OP.mult, OP.add))
                    p.vbar(V.tensor_scalar(sp32[:, :], sp32[:, :], 50.0, -50.0, OP.min, OP.max))
                    p.vbar(V.tensor_tensor(chunk(ymb, c), sp32[:, :], chunk(zsilu, c),
                                           OP.mult))
                if dbg:
                    for nm in list(dbg_h):
                        if nm.startswith(f"s{s}_"):
                            buf = {"xc": xcb, "delta": delta, "ub": ub, "ymb": ymb,
                                   "zs": zsilu, "xp": xpath, "pst": pstate,
                                   "aex": a_exp, "but": bu_t, "psc": pscan}[nm.split("_")[1]]
                            p.wait(V, "TE")
                            vt = p.done("V", V.tensor_copy(x_tm_f[:, 0:dbg[nm]],
                                                           buf[:, 0:dbg[nm]]))
                            p.wait(Q, "V", vt)
                            td = p.done("Qd", Q.dma_start(dbg_h[nm], x_tm_f[:, 0:dbg[nm]]))
                            Q.wait_ge(p.sem["Qd"], td)
                            p.wait(V, "Qd")
                for m in range(8):
                    for h in range(2):
                        b = 2 * m + h
                        if b == 0:
                            sw2_pre(dr["W_outa_s"], 0)
                        w = sw2_consume()
                        if b + 1 < 16:
                            sw2_pre(dr["W_outa_s"], b + 1)
                        for k8 in range(8):
                            k = h * 8 + k8
                            tt = p.done("TE", TE.matmul(psB[:, 0:TS],
                                                        w[:, k8 * 128:(k8 + 1) * 128],
                                                        chunk(ymb, k),
                                                        start=(k == 0), stop=(k == 15)))
                    p.wait(V, "TE", tt)
                    p.done("V", V.tensor_tensor(stream_c(m, sl), stream_c(m, sl),
                                                psB[:, 0:TS], OP.add))


        def ffn_phase(e0buf, e1buf, dump2, dump3, rowbuf, crow, browbuf01, colbuf):
            for s in range(NSL):
                sl = slice(s * TS, (s + 1) * TS)
                ln_slice(s, xn_c)
                sw2_pre(dr["gate_s"], 0, kw=64)
                w = sw2_consume()
                for k in range(8):
                    tt = p.done("TE", TE.matmul(psB[0:8, 0:TS], w[:, k * 8:(k + 1) * 8],
                                                xn_c(k),
                                                start=(k == 0), stop=(k == 7)))
                p.wait(V, "TE", tt)
                vt = p.done("V", V.tensor_copy(rowbuf[0:8, 0:TS], psB[0:8, 0:TS]))
                p.wait(TE, "V", vt)
                for it in range(TS // 128):
                    p.wait(TE, "V")
                    tt = p.done("TE", TE.transpose(psD[0:128, 0:8],
                                                   rowbuf[0:8, it * 128:(it + 1) * 128],
                                                   identf[0:8, 0:8]))
                    p.wait(V, "TE", tt)
                    p.done("V", V.tensor_copy(colbuf[:, it * 8:(it + 1) * 8],
                                              psD[0:128, 0:8]))
                for it in range(TS // 128):
                    lg = colbuf[:, it * 8:(it + 1) * 8]
                    base = 32 + it * 16
                    M1 = colbuf[:, base + 0:base + 1]
                    M2 = colbuf[:, base + 1:base + 2]
                    Z = colbuf[:, base + 2:base + 3]
                    rZ = colbuf[:, base + 3:base + 4]
                    eM2 = colbuf[:, base + 4:base + 5]
                    m0 = colbuf[:, base + 5:base + 6]
                    m1 = colbuf[:, base + 6:base + 7]
                    c0 = colbuf[:, base + 7:base + 8]
                    c1 = colbuf[:, base + 8:base + 9]
                    negM1 = colbuf[:, base + 9:base + 10]
                    lpb = colbuf[:, 96 + it * 8:96 + (it + 1) * 8]
                    p.vbar(V.tensor_reduce(M1, lg, mybir.AxisListType.X, OP.max))
                    p.vbar(V.tensor_scalar(lpb, lg, M1, None, OP.is_ge))
                    p.vbar(V.scalar_tensor_tensor(lpb, lpb, -1e30, lg, OP.mult, OP.add))
                    p.vbar(V.tensor_reduce(M2, lpb, mybir.AxisListType.X, OP.max))
                    p.vbar(V.tensor_scalar(negM1, M1, -1.0, None, OP.mult))
                    p.wait(S, "V")
                    stk = p.done("S", S.activation(colbuf[:, 224 + it * 8:224 + (it + 1) * 8],
                                                   lg, AF.Exp, bias=negM1, accum_out=Z))
                    p.wait(V, "S", stk)
                    p.vbar(V.reciprocal(rZ, Z))
                    p.vbar(V.tensor_tensor(eM2, M2, negM1, OP.add))
                    p.wait(S, "V")
                    stk = p.done("S", S.activation(eM2, eM2, AF.Exp))
                    p.wait(V, "S", stk)
                    p.vbar(V.tensor_scalar(m0, lg[:, 0:1], M1, None, OP.is_ge))
                    p.vbar(V.tensor_scalar(m1, lpb[:, 1:2], M2, None, OP.is_ge))
                    p.vbar(V.tensor_tensor(c0, rZ, m0, OP.mult))
                    p.vbar(V.tensor_tensor(eM2, eM2, rZ, OP.mult))
                    p.vbar(V.tensor_tensor(c1, eM2, m1, OP.mult))
                for it in range(TS // 128):
                    base = 32 + it * 16
                    p.wait(TE, "V")
                    tt = p.done("TE", TE.transpose(psD[0:2, 0:128],
                                                   colbuf[:, base + 7:base + 9],
                                                   identf[:, :]))
                    p.wait(V, "TE", tt)
                    p.done("V", V.tensor_copy(crow[0:2, it * 128:(it + 1) * 128],
                                              psD[0:2, 0:128]))
                p.wait(V, "V")
                vt = p.done("V", V.tensor_copy(browbuf01[0:2, 0:TS], crow[0:2, 0:TS]))
                p.wait(TE, "V", vt)
                p.done("TE", TE.matmul(psA[:, 1024:1024 + TS], pairsel[0:2, 0:128],
                                       browbuf01[0:2, 0:TS], start=True, stop=True))
                tt = p.done("TE", TE.matmul(psA[:, 1536:1536 + TS], pairsel[0:2, 128:256],
                                            browbuf01[0:2, 0:TS], start=True, stop=True))
                p.wait(V, "TE", tt)
                for m in range(8):
                    if m == 0:
                        sw2_pre(dr["w0_s"], 0)
                    w = sw2_consume()
                    sw2_pre(dr["w1_s"], m)
                    for k in range(8):
                        tt0 = p.done("TE", TE.matmul(psB[:, 0:TS], w[:, k * 128:(k + 1) * 128],
                                                     xn_c(k),
                                                     start=(k == 0), stop=(k == 7)))
                    p.wait(S, "TE", tt0)
                    stk = p.done("S", S.activation(e0buf[:, 0:TS], psB[:, 0:TS],
                                                   AF.Silu, bias=b0_col[:, m:m + 1]))
                    p.wait(TE, "S", stk)
                    w = sw2_consume()
                    if m + 1 < 8:
                        sw2_pre(dr["w0_s"], m + 1)
                    for k in range(8):
                        tt1 = p.done("TE", TE.matmul(psB[:, 0:TS], w[:, k * 128:(k + 1) * 128],
                                                     xn_c(k),
                                                     start=(k == 0), stop=(k == 7)))
                    p.wait(S, "TE", tt1)
                    stk = p.done("S", S.activation(e1buf[:, 0:TS], psB[:, 0:TS],
                                                   AF.Silu, bias=b1_col[:, m:m + 1]))
                    p.wait(TE, "S", stk)
                    p.wait(V, "S", stk)
                    p.vbar(V.tensor_tensor(dump2[:, :], e0buf[:, 0:TS],
                                           psA[:, 1024:1024 + TS], OP.mult))
                    p.vbar(V.tensor_tensor(dump3[:, :], e1buf[:, 0:TS],
                                           psA[:, 1536:1536 + TS], OP.mult))
                    p.vbar(V.tensor_tensor(dump2[:, :], dump2[:, :], dump3[:, :], OP.add))
                    p.done("V", V.tensor_tensor(stream_c(m, sl), stream_c(m, sl),
                                                dump2[:, :], OP.add))

        if n_phases >= 2:
            p.arena_reset()
            e0buf = p.aB(TS)
            e1buf = p.aB(TS)
            dump2 = p.aF(TS)
            dump3 = p.aF(TS)
            rowbuf = p.sb("rowbuf", [8, 2 * TS], F32)
            crow = p.sb("crow", [2, TS], F32)
            browbuf01 = p.sb("browbuf01", [2, TS], BF16)
            colbuf = p.sb("colbuf", [128, 256], F32)
            ffn_phase(e0buf, e1buf, dump2, dump3, rowbuf, crow, browbuf01, colbuf)


        if n_phases >= 3:
            p.arena_reset()
            NC8 = TS + 3
            updb = p.aB(8 * TS)
            cndb = p.aB(8 * TS)
            cab = p.aB(4 * TS)
            sab_ = p.aB(4 * TS)
            qbuf = p.aB(8 * NC8)
            kbuf = p.aB(8 * NC8)
            vbuf = p.aB(8 * NC8)
            ogb = p.aB(8 * TS)
            ment = p.aB(6 * 4 * TS)
            mtmp = p.aB(7 * 2 * TS)
            vecs = p.aB(8 * TS)
            reads_tm = p.aB(2 * 1024)
            mvfm = p.aB(16 * TS)
            cdiag3 = p.aB(32 * 128)
            dmA = p.aB(64); dgG = p.aB(64); nmat = p.aB(64); gmat = p.aB(64)
            ktm = p.aB(64); vtm = p.aB(64); xsol = p.aB(64)
            bdn = p.aB(128); bdnt = p.aB(128); bdn2 = p.aB(128); bdnt2 = p.aB(128)
            tmk = p.aB(128)
            scident = p.aB(128)
            sqtile = p.aB(1024)
            angf = p.aF(4 * TS)
            sp32 = p.aF(TS)
            dump2 = p.aF(TS)
            shalf = p.aF(TS)
            chalf = p.aF(TS)
            gatef = p.aF(TS)
            e1f = p.aF(64)

            def updc(c):
                return updb[:, c * TS:(c + 1) * TS]

            def qc(buf, c):
                return buf[:, c * NC8:(c + 1) * NC8]

            def me(i, c):
                return ment[:, (i * 4 + c) * TS:(i * 4 + c) * TS + TS]

            scum16 = p.sb("scum16", [16, TS], F32)
            spn16 = p.sb("spn16", [16, TS], F32)

            ab32 = p.sb("ab32", [128, TS], F32)
            alpha16 = p.sb("alpha16", [16, TS], F32)
            beta16 = p.sb("beta16", [16, TS], F32)
            banr = p.sb("banr", [16, TS], F32)
            scstate = x_tm_f[0:16, 944:945]
            scprev = x_tm_f[0:16, 945:946]
            scum_row = p.sb("scum_row", [1, 16 * 64], F32)
            batchA = x_tm_f[:, 946:954]
            tmc = x_tm_f[0:64, 954:970]
            batchR = x_tm_f[:, 272:280]
            batchS = x_tm_f[:, 280:288]
            batchG = x_tm_f[:, 288:296]
            batchGB = x_tm_f[:, 296:304]
            batchBA = x_tm_f[:, 304:312]
            batchB = x_tm_f[:, 312:320]
            row16 = x_tm_f[0:16, 320:384]
            row16b = x_tm_f[0:16, 384:448]
            row16t = x_tm_f[0:16, 0:TS]
            row16tb = x_tm_b[0:16, 0:TS]
            gl16 = x_tm_f[0:1, TS:TS + 16]
            hsum = x_tm_f[:, 512:528]
            paircol = x_tm_f[:, 528:529]
            qcw_col = x_tm_f[:, 529:625]
            s0p = x_tm_b[:, 256:768]
            vstate = x_tm_f[:, 936:944]
            mbias_a = x_tm_f[:, 640:704]
            mbias_g = x_tm_f[:, 704:768]
            wblk_col = x_tm_f[:, 768:784]
            rmsm_col = x_tm_f[:, 784:792]
            rmsv_col = x_tm_f[:, 792:800]
            boutb_col = x_tm_f[:, 800:808]
            bctrl_col = x_tm_f[:, 808:861]
            onesf_row = x_tm_f[0:1, 861:925]
            cmhalfpi = x_tm_f[:, 925:926]
            stg = x_tm_f[:, 928:932]
            stg16 = x_tm_f[0:16, 932:936]

            for tgt, src in [(mbias_a, dr["MBIAS_A"]), (mbias_g, dr["MBIAS_G"]),
                             (wblk_col, dr["W_block_l"]),
                             (rmsm_col, dr["rms_m_l"]), (rmsv_col, dr["rms_v_perm"]),
                             (boutb_col, dr["b_outb_l"]), (bctrl_col, dr["b_ctrl_cols"])]:
                p.done("Qd", Q.dma_start(tgt[:, :], src))
            p.done("Qd", Q.dma_start(qcw_col[:, 0:32], dr["qconv_l"]))
            p.done("Qd", Q.dma_start(qcw_col[:, 32:64], dr["kconv_l"]))
            p.done("Qd", Q.dma_start(qcw_col[:, 64:96], dr["vconv_l"]))
            p.wait(V, "Qd")
            V.memset(onesf_row[0:1, :], 1.0)
            V.memset(cmhalfpi[:, :], -np.pi / 2)
            V.memset(s0p[:, :], 0.0)
            V.memset(vstate[:, :], 0.0)
            V.memset(scstate[:, :], 0.0)
            V.memset(scprev[:, :], 0.0)
            for c in range(8):
                V.memset(qc(qbuf, c)[:, 0:3], 0.0)
                V.memset(qc(kbuf, c)[:, 0:3], 0.0)
                p.done("V", V.memset(qc(vbuf, c)[:, 0:3], 0.0))

            def compose(dst_t0, dst_step, src_t0, src_step, n, bar):
                ME = ment.rearrange("p (i c t) -> p i c t", i=6, c=4)
                A = [ME[:, i, :, dst_t0:dst_t0 + dst_step * (n - 1) + 1:dst_step]
                     for i in range(6)]
                B = [ME[:, i, :, src_t0:src_t0 + src_step * (n - 1) + 1:src_step]
                     for i in range(6)]
                MT = mtmp.rearrange("p (i t) -> p i t", i=7)
                Tt = [MT[:, i, 0:4 * n].rearrange("p (c t) -> p c t", c=4) for i in range(7)]

                def op(ins):
                    if bar:
                        p.vbar(ins)
                    else:
                        p.done("V", ins)

                for (o, a0, b0, a1, b1) in ((0, 0, 0, 1, 2), (1, 0, 1, 1, 3),
                                            (2, 2, 0, 3, 2), (3, 2, 1, 3, 3),
                                            (4, 0, 4, 1, 5), (5, 2, 4, 3, 5)):
                    op(V.tensor_tensor(Tt[o], A[a0], B[b0], OP.mult))
                    op(V.tensor_tensor(Tt[6], A[a1], B[b1], OP.mult))
                    op(V.tensor_tensor(Tt[o], Tt[o], Tt[6], OP.add))
                    if o >= 4:
                        op(V.tensor_tensor(Tt[o], Tt[o], A[o], OP.add))
                p.vbar(V.tensor_copy(Tt[6][:, :, 0:n] if False else MT[:, 6, 0:4 * n],
                                     MT[:, 6, 0:4 * n]))
                for i in range(6):
                    op(V.tensor_copy(A[i], Tt[i]))
                p.vbar(V.tensor_copy(MT[:, 6, 0:4], MT[:, 6, 0:4]))

            for s in range(NSL):
                sl = slice(s * TS, (s + 1) * TS)
                ln_slice(s, xn_c)
                for m in range(53):
                    if m == 0:
                        sw2_pre(dr["W_ctrl_s"], 0)
                    w = sw2_consume()
                    if m + 1 < 53:
                        sw2_pre(dr["W_ctrl_s"], m + 1)
                    for k in range(8):
                        tt = p.done("TE", TE.matmul(psB[:, 0:TS], w[:, k * 128:(k + 1) * 128],
                                                    xn_c(k),
                                                    start=(k == 0), stop=(k == 7)))
                    bcol = bctrl_col[:, m:m + 1]
                    if m < 8:
                        p.wait(S, "TE", tt)
                        stk = p.done("S", S.activation(updc(m), psB[:, 0:TS], AF.Sigmoid,
                                                       bias=bcol))
                        p.wait(TE, "S", stk)
                    elif m < 12:
                        p.wait(V, "TE", tt)
                        vt = p.done("V", V.tensor_scalar(angf[:, (m - 8) * TS:(m - 7) * TS],
                                                         psB[:, 0:TS], bcol, None, OP.add))
                        p.wait(TE, "V", vt)
                    elif m < 20:
                        p.wait(S, "TE", tt)
                        stk = p.done("S", S.activation(cndb[:, (m - 12) * TS:(m - 11) * TS],
                                                       psB[:, 0:TS], AF.Silu, bias=bcol))
                        p.wait(TE, "S", stk)
                    elif m < 44:
                        buf = qbuf if m < 28 else (kbuf if m < 36 else vbuf)
                        cc = (m - 20) % 8
                        p.wait(V, "TE", tt)
                        vt = p.done("V", V.tensor_scalar(qc(buf, cc)[:, 3:3 + TS],
                                                         psB[:, 0:TS], bcol, None, OP.add))
                        p.wait(TE, "V", vt)
                    elif m == 44:
                        p.wait(V, "TE", tt)
                        vt = p.done("V", V.tensor_scalar(ab32[:, :], psB[:, 0:TS], bcol,
                                                         None, OP.add))
                        p.wait(TE, "V", vt)
                    else:
                        p.wait(V, "TE", tt)
                        vt = p.done("V", V.tensor_scalar(ogb[:, (m - 45) * TS:(m - 44) * TS],
                                                         psB[:, 0:TS], bcol, None, OP.add))
                        p.wait(TE, "V", vt)
                p.wait(S, "V")
                S.activation(alpha16[:, :], ab32[0:16, :], AF.Sigmoid)
                stk = p.done("S", S.activation(beta16[:, :], ab32[64:80, :], AF.Sigmoid))
                p.sbar(S.activation(spn16[:, :], ab32[0:16, :], AF.Exp, scale=-1.0))
                stk = p.done("S", S.activation(spn16[:, :], spn16[:, :], AF.Ln,
                                               bias=ccol["one"][0:16, 0:1]))
                p.wait(V, "S", stk)
                p.vbar(V.tensor_tensor_scan(scum16[:, :],
                                            ccol["zero"][0:16, 0:1].broadcast_to([16, TS]),
                                            spn16[:, :], scstate[0:16, 0:1], OP.add, OP.add))
                p.vbar(V.tensor_copy(stg16[0:16, 0:2], scum16[:, TS - 2:TS]))
                p.vbar(V.tensor_copy(scstate[0:16, 0:1], stg16[0:16, 1:2]))
                p.vbar(V.reciprocal(banr[:, :], alpha16[:, :]))
                p.vbar(V.tensor_tensor(banr[:, :], banr[:, :], beta16[:, :], OP.mult))
                p.vbar(V.tensor_scalar(banr[:, :], banr[:, :], -1.0, None, OP.mult))
                for which, buf in ((0, qbuf), (1, kbuf), (2, vbuf)):
                    p.wait(V, "S")
                    p.wait(V, "TE")
                    for c in range(8):
                        for j in range(4):
                            p.done("V", V.tensor_scalar(
                                cdiag3[:, (c * 4 + j) * 128:(c * 4 + j + 1) * 128],
                                ident[:, :], qcw_col[:, which * 32 + c * 4 + j:
                                                     which * 32 + c * 4 + j + 1],
                                None, OP.mult))
                    p.wait(TE, "V")
                    for c in range(8):
                        for j in range(4):
                            off = 3 - j
                            tt = p.done("TE", TE.matmul(
                                psB[:, 0:TS],
                                cdiag3[:, (c * 4 + j) * 128:(c * 4 + j + 1) * 128],
                                qc(buf, c)[:, 3 - off:3 - off + TS],
                                start=(j == 0), stop=(j == 3)))
                        p.wait(V, "TE", tt)
                        p.vbar(V.tensor_copy(stg[:, 0:3], qc(buf, c)[:, TS:TS + 3]))
                        vt = p.done("V", V.tensor_copy(qc(buf, c)[:, 0:3], stg[:, 0:3]))
                        p.wait(S, "V", vt)
                        stk = p.done("S", S.activation(qc(buf, c)[:, 3:3 + TS], psB[:, 0:TS],
                                                       AF.Silu))
                        p.wait(TE, "S", stk)
                        p.wait(V, "S", stk)
                for which, buf in ((0, qbuf), (1, kbuf)):
                    for c in range(8):
                        p.wait(S, "TE")
                        stk = p.done("S", S.activation(x2b[:, :], qc(buf, c)[:, 3:3 + TS],
                                                       AF.Square))
                        p.wait(TE, "S", stk)
                        tt = p.done("TE", TE.matmul(psC[0:16, 0:TS],
                                                    selq8[:, c * 16:(c + 1) * 16],
                                                    x2b[:, :], start=(c == 0), stop=(c == 7)))
                    p.wait(V, "TE", tt)
                    p.vbar(V.tensor_scalar(row16t[:, 0:TS], psC[0:16, 0:TS], 1.0, 1e-12,
                                           OP.mult, OP.add))
                    vt = p.tick["V"]
                    p.wait(S, "V", vt)
                    p.sbar(S.activation(row16t[0:16, 0:TS], row16t[0:16, 0:TS], AF.Sqrt))
                    p.wait(V, "S")
                    p.vbar(V.reciprocal(row16t[:, 0:TS], row16t[:, 0:TS]))
                    vt = p.done("V", V.tensor_copy(row16tb[:, 0:TS], row16t[:, 0:TS]))
                    p.wait(TE, "V", vt)
                    for c in range(8):
                        tt = p.done("TE", TE.matmul(psC[:, 0:TS],
                                                    selqe[:, c * 128:(c + 1) * 128],
                                                    row16tb[:, 0:TS], start=True, stop=True))
                        p.wait(V, "TE", tt)
                        p.vbar(V.tensor_tensor(qc(buf, c)[:, 3:3 + TS],
                                               qc(buf, c)[:, 3:3 + TS], psC[:, 0:TS],
                                               OP.mult))
                        p.wait(TE, "V")
                for c in range(4):
                    ac = angf[:, c * TS:(c + 1) * TS]
                    p.vbar(V.tensor_scalar(sp32[:, :], ac, 1.0 / (2 * np.pi), None, OP.mult))
                    p.vbar(V.tensor_scalar(dump2[:, :], sp32[:, :], 0.5, None, OP.is_ge))
                    for kthr in (1.5, 2.5):
                        p.vbar(V.tensor_scalar(gatef[:, :], sp32[:, :], kthr, None,
                                               OP.is_ge))
                        p.vbar(V.tensor_tensor(dump2[:, :], dump2[:, :], gatef[:, :],
                                               OP.add))
                    for kthr in (-0.5, -1.5, -2.5):
                        p.vbar(V.tensor_scalar(gatef[:, :], sp32[:, :], kthr, None,
                                               OP.is_le))
                        p.vbar(V.tensor_tensor(dump2[:, :], dump2[:, :], gatef[:, :],
                                               OP.subtract))
                    p.vbar(V.scalar_tensor_tensor(sp32[:, :], dump2[:, :], -2 * np.pi, ac,
                                                  OP.mult, OP.add))
                    p.vbar(V.tensor_scalar(sp32[:, :], sp32[:, :], np.pi, -np.pi,
                                           OP.min, OP.max))
                    p.vbar(V.tensor_scalar(dump2[:, :], sp32[:, :], 0.5, -np.pi / 2,
                                           OP.mult, OP.add))
                    vt = p.tick["V"]
                    p.wait(S, "V", vt)
                    S.activation(shalf[:, :], dump2[:, :], AF.Sin)
                    stk = p.done("S", S.activation(chalf[:, :], sp32[:, :], AF.Sin, scale=0.5))
                    p.wait(V, "S", stk)
                    p.vbar(V.tensor_tensor(sp32[:, :], chalf[:, :], chalf[:, :], OP.mult))
                    p.vbar(V.tensor_scalar(cab[:, c * TS:(c + 1) * TS], sp32[:, :], -2.0, 1.0,
                                           OP.mult, OP.add))
                    p.vbar(V.tensor_tensor(sp32[:, :], chalf[:, :], shalf[:, :], OP.mult))
                    p.vbar(V.tensor_scalar(sab_[:, c * TS:(c + 1) * TS], sp32[:, :], -2.0,
                                           None, OP.mult))
                for c in range(4):
                    cav = cab[:, c * TS:(c + 1) * TS]
                    sav = sab_[:, c * TS:(c + 1) * TS]
                    u1 = updc(c)
                    u2 = updc(4 + c)
                    wc4 = [wblk_col[:, c * 4 + i:c * 4 + i + 1] for i in range(4)]
                    for (o, wa, wb, uu, sgn) in ((0, 0, 1, u1, -1.0), (1, 2, 3, u1, -1.0),
                                                 (2, 0, 1, u2, 1.0), (3, 2, 3, u2, 1.0)):
                        if o < 2:
                            p.vbar(V.tensor_scalar(sp32[:, :], cav, wc4[wa], None, OP.mult))
                            p.vbar(V.tensor_scalar(dump2[:, :], sav, wc4[wb], None, OP.mult))
                            p.vbar(V.tensor_tensor(sp32[:, :], sp32[:, :], dump2[:, :],
                                                   OP.subtract))
                        else:
                            p.vbar(V.tensor_scalar(sp32[:, :], sav, wc4[wa], None, OP.mult))
                            p.vbar(V.tensor_scalar(dump2[:, :], cav, wc4[wb], None, OP.mult))
                            p.vbar(V.tensor_tensor(sp32[:, :], sp32[:, :], dump2[:, :],
                                                   OP.add))
                        p.vbar(V.tensor_tensor(me(o, c), uu, sp32[:, :], OP.mult))
                    p.vbar(V.tensor_scalar(sp32[:, :], u1, -1.0, 1.0, OP.mult, OP.add))
                    p.vbar(V.tensor_tensor(me(4, c), sp32[:, :],
                                           cndb[:, c * TS:(c + 1) * TS], OP.mult))
                    p.vbar(V.tensor_scalar(sp32[:, :], u2, -1.0, 1.0, OP.mult, OP.add))
                    p.vbar(V.tensor_tensor(me(5, c), sp32[:, :],
                                           cndb[:, (4 + c) * TS:(5 + c) * TS], OP.mult))
                lev = 1
                while 2 * lev <= TS:
                    n = TS // (2 * lev)
                    compose(2 * lev - 1, 2 * lev, lev - 1, 2 * lev, n, bar=True)
                    lev *= 2
                lev = TS // 4
                while lev >= 1:
                    n = TS // (2 * lev) - 1
                    if n > 0:
                        compose(3 * lev - 1, 2 * lev, 2 * lev - 1, 2 * lev, n, bar=True)
                    lev //= 2
                for c in range(4):
                    v1c = vstate[:, c:c + 1]
                    v2c = vstate[:, 4 + c:4 + c + 1]
                    p.vbar(V.tensor_scalar(sp32[:, :], me(0, c), v1c, None, OP.mult))
                    p.vbar(V.scalar_tensor_tensor(dump2[:, :], me(1, c), v2c, sp32[:, :],
                                                  OP.mult, OP.add))
                    p.vbar(V.tensor_tensor(vecs[:, c * TS:(c + 1) * TS], dump2[:, :],
                                           me(4, c), OP.add))
                    p.vbar(V.tensor_scalar(sp32[:, :], me(2, c), v1c, None, OP.mult))
                    p.vbar(V.scalar_tensor_tensor(dump2[:, :], me(3, c), v2c, sp32[:, :],
                                                  OP.mult, OP.add))
                    p.vbar(V.tensor_tensor(vecs[:, (4 + c) * TS:(5 + c) * TS], dump2[:, :],
                                           me(5, c), OP.add))
                for c in range(8):
                    p.vbar(V.tensor_copy(stg[:, 0:2],
                                         vecs[:, c * TS + TS - 2:c * TS + TS]))
                    p.vbar(V.tensor_copy(vstate[:, c:c + 1], stg[:, 1:2]))

                for c in range(4):
                    rb = scum16[:, c * 64 - 1:c * 64] if c > 0 else scprev[0:16, 0:1]
                    p.vbar(V.tensor_scalar(row16[:, 0:64], scum16[:, c * 64:(c + 1) * 64],
                                           rb, None, OP.subtract))
                    p.vbar(V.tensor_scalar(row16b[:, 0:1], row16[:, 63:64], -1.0, None,
                                           OP.mult))
                    vt = p.tick["V"]
                    p.wait(S, "V", vt)
                    stk = p.done("S", S.activation(row16b[:, 0:1], row16b[:, 0:1], AF.Exp))
                    p.wait(TE, "S", stk)
                    tt = p.done("TE", TE.transpose(psD[0:1, 0:16], row16b[:, 0:1],
                                                   identf[0:16, 0:16]))
                    p.wait(V, "TE", tt)
                    p.vbar(V.tensor_copy(gl16[0:1, 0:16], psD[0:1, 0:16]))
                    for (rows, dstb) in ((scum16[:, c * 64:(c + 1) * 64], batchA),
                                         (row16[:, 0:64], batchR),
                                         (spn16[:, c * 64:(c + 1) * 64], batchS),
                                         (banr[:, c * 64:(c + 1) * 64], batchBA),
                                         (beta16[:, c * 64:(c + 1) * 64], batchB)):
                        p.wait(TE, "V")
                        tt = p.done("TE", TE.transpose(psD[0:64, 0:16], rows[0:16, :],
                                                       identf[0:16, 0:16]))
                        p.wait(V, "TE", tt)
                        vt = p.done("V", V.tensor_copy(tmc[0:64, 0:16], psD[0:64, 0:16]))
                        p.wait(TE, "V", vt)
                        TE.matmul(psC[0:64, 0:8], identf[0:64, 0:64], tmc[0:64, 0:16:2],
                                  start=True, stop=True)
                        tt = p.done("TE", TE.matmul(psC[64:128, 0:8], identf[0:64, 0:64],
                                                    tmc[0:64, 1:16:2], start=True, stop=True))
                        p.wait(V, "TE", tt)
                        p.vbar(V.tensor_copy(dstb[:, 0:8], psC[:, 0:8]))
                        if dstb is batchA:
                            for h in range(16):
                                p.wait(TE, "V")
                                tt = p.done("TE", TE.transpose(psD[0:1, 0:64],
                                                               tmc[0:64, h:h + 1],
                                                               identf[0:64, 0:64]))
                                p.wait(V, "TE", tt)
                                p.done("V", V.tensor_copy(scum_row[0:1, h * 64:(h + 1) * 64],
                                                          psD[0:1, 0:64]))
                            p.vbar(V.tensor_copy(scum_row[0:1, 0:2], scum_row[0:1, 0:2]))
                    p.vbar(V.tensor_scalar(batchG[:, :], batchR[:, :], -1.0, None, OP.mult))
                    p.vbar(V.tensor_tensor(batchGB[:, :], batchS[:, :], batchR[:, :],
                                           OP.subtract))
                    vt = p.tick["V"]
                    p.wait(S, "V", vt)
                    S.activation(batchG[:, :], batchG[:, :], AF.Exp)
                    stk = p.done("S", S.activation(batchGB[:, :], batchGB[:, :], AF.Exp))
                    p.wait(V, "S", stk)
                    p.wait(TE, "S", stk)
                    for j in range(8):
                        hc1, ho1 = j, 0
                        hc2, ho2 = j, 64
                        bo = j * 64
                        p.wait(TE, "V")
                        TE.matmul(psD[0:64, 0:64], onesf_row[0:1, 0:64],
                                  scum_row[0:1, (2 * j) * 64:(2 * j + 1) * 64],
                                  start=True, stop=True)
                        tt = p.done("TE", TE.matmul(psD[64:128, 0:64], onesf_row[0:1, 0:64],
                                                    scum_row[0:1, (2 * j + 1) * 64:(2 * j + 2) * 64],
                                                    start=True, stop=True))
                        p.wait(V, "TE", tt)
                        p.vbar(V.tensor_tensor(e1f[:, 0:64], psD[:, 0:64],
                                               batchA[:, j:j + 1].broadcast_to([128, 64]),
                                               OP.subtract))
                        p.vbar(V.tensor_tensor(dump2[:, 0:64], e1f[:, 0:64], mbias_a[:, :],
                                               OP.add))
                        p.vbar(V.scalar_tensor_tensor(sp32[:, 0:64], e1f[:, 0:64], -1.0,
                                                      mbias_g[:, :], OP.mult, OP.add))
                        vt = p.tick["V"]
                        p.wait(S, "V", vt)
                        S.activation(dmA[:, :], dump2[:, 0:64], AF.Exp)
                        stk = p.done("S", S.activation(dgG[:, :], sp32[:, 0:64], AF.Exp))
                        p.wait(V, "S", stk)
                        kap1 = qc(kbuf, j)[0:64, 3 + c * 64:3 + (c + 1) * 64]
                        kap2 = qc(kbuf, j)[64:128, 3 + c * 64:3 + (c + 1) * 64]
                        qap1 = qc(qbuf, j)[0:64, 3 + c * 64:3 + (c + 1) * 64]
                        qap2 = qc(qbuf, j)[64:128, 3 + c * 64:3 + (c + 1) * 64]
                        p.wait(TE, "V")
                        TE.matmul(psA[0:64, 0:64], kap1, kap1, start=True, stop=True)
                        tt = p.done("TE", TE.matmul(psA[64:128, 0:64], kap2, kap2,
                                                    start=True, stop=True))
                        TE.matmul(psA[0:64, 1024:1088], kap1, qap1, start=True, stop=True)
                        tt = p.done("TE", TE.matmul(psA[64:128, 1024:1088], kap2, qap2,
                                                    start=True, stop=True))
                        p.wait(V, "TE", tt)
                        p.vbar(V.tensor_tensor(nmat[:, :], dmA[:, :], psA[:, 0:64],
                                               OP.mult))
                        p.vbar(V.tensor_scalar(nmat[:, :], nmat[:, :], batchBA[:, j:j + 1],
                                               None, OP.mult))
                        p.vbar(V.tensor_tensor(gmat[:, :], dgG[:, :],
                                               psA[:, 1024:1088], OP.mult))
                        for (srcb, dstt) in ((kbuf, ktm), (vbuf, vtm)):
                            p.wait(TE, "V")
                            tt = p.done("TE", TE.transpose(
                                psE[0:64, 0:128],
                                qc(srcb, j)[:, 3 + c * 64:3 + (c + 1) * 64], ident[:, :]))
                            p.wait(V, "TE", tt)
                            vt = p.done("V", V.tensor_copy(tmk[0:64, :], psE[0:64, 0:128]))
                            if dbg and s == 0 and c == 0 and j == 0 and "mach" in dbg \
                                    and dstt is ktm and dbg["mach"] > 544:
                                vt = p.done("V", V.tensor_copy(wstage[0:64, 640:768],
                                                               tmk[0:64, :]))
                            p.wait(TE, "V", vt)
                            TE.matmul(psC[0:64, 0:64], ident[0:64, 0:64], tmk[0:64, 0:64],
                                      start=True, stop=True)
                            tt = p.done("TE", TE.matmul(psC[64:128, 0:64], ident[0:64, 0:64],
                                                        tmk[0:64, 64:128],
                                                        start=True, stop=True))
                            p.wait(V, "TE", tt)
                            p.vbar(V.tensor_copy(dstt[:, :], psC[:, 0:64]))
                        p.wait(TE, "V")
                        TE.matmul(psC[0:64, 0:64],
                                  qc(kbuf, hc1)[ho1:ho1 + 64, 3 + c * 64:3 + (c + 1) * 64],
                                  s0p[0:64, j * 64:(j + 1) * 64], start=True, stop=True)
                        tt = p.done("TE", TE.matmul(
                            psC[64:128, 0:64],
                            qc(kbuf, hc2)[ho2:ho2 + 64, 3 + c * 64:3 + (c + 1) * 64],
                            s0p[64:128, j * 64:(j + 1) * 64], start=True, stop=True))
                        p.wait(V, "TE", tt)
                        p.vbar(V.tensor_scalar(dump2[:, 0:64], psC[:, 0:64],
                                               batchGB[:, j:j + 1], None, OP.mult))
                        p.vbar(V.tensor_tensor(dump2[:, 0:64], vtm[:, :], dump2[:, 0:64],
                                               OP.subtract))
                        p.vbar(V.tensor_scalar(xsol[:, :], dump2[:, 0:64],
                                               batchB[:, j:j + 1], None, OP.mult))
                        p.vbar(V.memset(bdn[:, :], 0.0))
                        V.tensor_copy(bdn[0:64, 0:64], nmat[0:64, :])
                        p.vbar(V.tensor_copy(bdn[64:128, 64:128], nmat[64:128, :]))
                        p.wait(TE, "V")
                        tt = p.done("TE", TE.transpose(psE[:, 0:128], bdn[:, :], ident[:, :]))
                        p.wait(V, "TE", tt)
                        p.vbar(V.tensor_copy(bdnt[:, :], psE[:, 0:128]))
                        pw, pwt = bdn, bdnt
                        alt, altt = bdn2, bdnt2
                        for ksq in range(6):
                            p.wait(TE, "V")
                            TE.matmul(psC[:, 0:64], pwt[:, :], xsol[:, :],
                                      start=True, stop=False)
                            tt = p.done("TE", TE.matmul(psC[:, 0:64], ident[:, :],
                                                        xsol[:, :], start=False, stop=True))
                            p.wait(S, "TE", tt)
                            stk = p.done("S", S.activation(xsol[:, :], psC[:, 0:64], AF.Copy))
                            p.wait(TE, "S", stk)
                            p.wait(V, "S", stk)
                            if ksq < 5:
                                TE.matmul(psC[:, 0:128], pwt[:, :], pw[:, :],
                                          start=True, stop=True)
                                tt = p.done("TE", TE.matmul(psD[:, 0:128], pw[:, :],
                                                            pwt[:, :], start=True, stop=True))
                                p.wait(V, "TE", tt)
                                p.vbar(V.tensor_copy(alt[:, :], psC[:, 0:128]))
                                p.vbar(V.tensor_copy(altt[:, :], psD[:, 0:128]))
                                pw, alt = alt, pw
                                pwt, altt = altt, pwt
                        if dbg and s == 0 and c == 0 and j == 0 and "mach" in dbg:
                            p.wait(V, "TE")
                            p.wait(V, "S")
                            for off, bb in ((0, nmat), (64, gmat), (128, xsol), (192, ktm),
                                            (256, vtm), (384, dmA), (448, dgG)):
                                p.done("V", V.tensor_copy(wstage[:, off:off + 64], bb[:, :]))
                            vt = p.done("V", V.tensor_copy(wstage[:, 320:384], e1f[:, 0:64]))
                            vt = p.done("V", V.tensor_copy(wstage[:, 512:520], batchA[:, 0:8]))
                            vt = p.done("V", V.tensor_copy(wstage[:, 520:528], batchR[:, 0:8]))
                            vt = p.done("V", V.tensor_copy(wstage[:, 528:536], batchGB[:, 0:8]))
                            vt = p.done("V", V.tensor_copy(wstage[:, 536:544], batchBA[:, 0:8]))
                            if dbg["mach"] > 544:
                                p.done("V", V.tensor_copy(wstage[:, 576:640],
                                                          qc(kbuf, 0)[:, 3:67]))
                                p.done("V", V.tensor_copy(wstage[0:64, 768:784],
                                                          tmc[0:64, 0:16]))
                                p.done("V", V.tensor_copy(wstage[0:1, 784:912],
                                                          scum_row[0:1, 0:128]))
                                p.done("V", V.tensor_copy(wstage[:, 912:976],
                                                          psA[:, 0:64]))
                                vt = p.done("V", V.tensor_copy(wstage[:, 976:1024],
                                                               psA[:, 1024:1072]))
                            p.wait(Q, "V", vt)
                            td = p.done("Qd", Q.dma_start(dbg_h["mach"], wstage[:, 0:dbg["mach"]]))
                            Q.wait_ge(p.sem["Qd"], td)
                            p.wait(V, "Qd")
                        p.wait(TE, "V")
                        TE.matmul(psC[0:64, 0:64], gmat[0:64, :], xsol[0:64, :],
                                  start=True, stop=True)
                        tt = p.done("TE", TE.matmul(psC[64:128, 0:64], gmat[64:128, :],
                                                    xsol[64:128, :], start=True, stop=True))
                        TE.matmul(psD[0:64, 0:64],
                                  qc(qbuf, hc1)[ho1:ho1 + 64, 3 + c * 64:3 + (c + 1) * 64],
                                  s0p[0:64, j * 64:(j + 1) * 64], start=True, stop=True)
                        tt2 = p.done("TE", TE.matmul(
                            psD[64:128, 0:64],
                            qc(qbuf, hc2)[ho2:ho2 + 64, 3 + c * 64:3 + (c + 1) * 64],
                            s0p[64:128, j * 64:(j + 1) * 64], start=True, stop=True))
                        p.wait(V, "TE", tt2)
                        p.vbar(V.tensor_scalar(dump2[:, 0:64], psD[:, 0:64],
                                               batchG[:, j:j + 1], None, OP.mult))
                        p.vbar(V.tensor_tensor(dump2[:, 0:64], dump2[:, 0:64], psC[:, 0:64],
                                               OP.add))
                        ti2 = (c * 64) // 128
                        tof2 = (c * 64) % 128
                        rtm = reads_tm[:, ti2 * 1024:(ti2 + 1) * 1024]
                        V.tensor_copy(rtm[tof2:tof2 + 64, (2 * j) * 64:(2 * j + 1) * 64],
                                      dump2[0:64, 0:64])
                        p.vbar(V.tensor_copy(rtm[tof2:tof2 + 64, (2 * j + 1) * 64:(2 * j + 2) * 64],
                                             dump2[64:128, 0:64]))
                        p.wait(TE, "V")
                        TE.matmul(psD[0:64, 0:1], onesf_row[0:1, 0:64],
                                  gl16[0:1, 2 * j:2 * j + 1], start=True, stop=True)
                        tt = p.done("TE", TE.matmul(psD[64:128, 0:1], onesf_row[0:1, 0:64],
                                                    gl16[0:1, 2 * j + 1:2 * j + 2],
                                                    start=True, stop=True))
                        p.wait(V, "TE", tt)
                        p.vbar(V.tensor_scalar(scident[:, :], ident[:, :], psD[:, 0:1],
                                               None, OP.mult))
                        p.vbar(V.reciprocal(stg[:, 0:1], batchG[:, j:j + 1]))
                        p.vbar(V.tensor_tensor(stg[:, 1:2], stg[:, 0:1], psD[:, 0:1],
                                               OP.mult))
                        p.vbar(V.tensor_scalar(ktm[:, :], ktm[:, :], stg[:, 1:2],
                                               None, OP.mult))
                        p.wait(TE, "V")
                        TE.matmul(psB[0:64, 0:64], scident[0:64, 0:64],
                                  s0p[0:64, j * 64:(j + 1) * 64], start=True, stop=False)
                        TE.matmul(psB[0:64, 0:64], ktm[0:64, :], xsol[0:64, :],
                                  start=False, stop=True)
                        TE.matmul(psB[64:128, 0:64], scident[64:128, 64:128],
                                  s0p[64:128, j * 64:(j + 1) * 64], start=True, stop=False)
                        tt = p.done("TE", TE.matmul(psB[64:128, 0:64], ktm[64:128, :],
                                                    xsol[64:128, :], start=False, stop=True))
                        p.wait(V, "TE", tt)
                        p.vbar(V.tensor_copy(s0p[:, j * 64:(j + 1) * 64], psB[:, 0:64]))
                p.vbar(V.tensor_copy(stg16[0:16, 0:2], scum16[:, TS - 2:TS]))
                p.vbar(V.tensor_copy(scprev[0:16, 0:1], stg16[0:16, 1:2]))
                if dbg:
                    for nm in list(dbg_h):
                        if nm.startswith(f"b{s}_"):
                            buf = {"reads": reads_tm, "vecs": vecs, "scum": scum16,
                                   "spn": spn16, "ab": ab32, "upd": updb, "cnd": cndb,
                                   "ca": cab, "sa": sab_, "qq": qbuf, "kk": kbuf,
                                   "vv": vbuf, "s0": s0p, "og": ogb,
                                   "ment": ment}[nm.split("_")[1]]
                            wid = dbg[nm]
                            nparts = {"scum": 16, "spn": 16}.get(nm.split("_")[1], 128)
                            p.wait(V, "TE")
                            p.wait(V, "S")
                            vt = p.done("V", V.tensor_copy(wstage[0:nparts, 0:wid],
                                                           buf[0:nparts, 0:wid]))
                            p.wait(Q, "V", vt)
                            td = p.done("Qd", Q.dma_start(dbg_h[nm], wstage[:, 0:wid]))
                            Q.wait_ge(p.sem["Qd"], td)
                            p.wait(V, "Qd")
                for ti2 in range(2):
                    rtm = reads_tm[:, ti2 * 1024:(ti2 + 1) * 1024]
                    p.vbar(V.tensor_tensor(sqtile[:, :], rtm[:, :], rtm[:, :], OP.mult))
                    p.vbar(V.tensor_reduce(hsum[:, 0:16],
                                           sqtile.rearrange("p (h d) -> p h d", h=16),
                                           mybir.AxisListType.X, OP.add))
                    p.vbar(V.tensor_scalar(hsum[:, 0:16], hsum[:, 0:16], 1.0 / 64, 1e-6,
                                           OP.mult, OP.add))
                    vt = p.tick["V"]
                    p.wait(S, "V", vt)
                    p.sbar(S.activation(hsum[:, 0:16], hsum[:, 0:16], AF.Sqrt))
                    p.wait(V, "S")
                    p.vbar(V.reciprocal(hsum[:, 0:16], hsum[:, 0:16]))
                    p.vbar(V.tensor_tensor(
                        rtm.rearrange("p (h d) -> p h d", h=16),
                        rtm.rearrange("p (h d) -> p h d", h=16),
                        hsum.rearrange("p (h o) -> p h o", o=1).broadcast_to([128, 16, 64]),
                        OP.mult))
                for ti2 in range(2):
                    for cc in range(8):
                        p.wait(TE, "V")
                        tt = p.done("TE", TE.transpose(
                            psE[:, 0:128],
                            reads_tm[:, ti2 * 1024 + cc * 128:ti2 * 1024 + (cc + 1) * 128],
                            ident[:, :]))
                        p.wait(V, "TE", tt)
                        p.done("V", V.tensor_copy(
                            mvfm[:, cc * TS + ti2 * 128:cc * TS + (ti2 + 1) * 128],
                            psE[:, 0:128]))
                p.wait(V, "V")
                for cc in range(8):
                    p.done("V", V.tensor_scalar(mvfm[:, cc * TS:(cc + 1) * TS],
                                                mvfm[:, cc * TS:(cc + 1) * TS],
                                                rmsm_col[:, cc:cc + 1], None, OP.mult))
                tt = p.tick["TE"]
                for cc in range(8):
                    p.wait(S, "V")
                    p.wait(S, "TE", tt)
                    stk = p.done("S", S.activation(x2b[:, :], vecs[:, cc * TS:(cc + 1) * TS],
                                                   AF.Square))
                    p.wait(TE, "S", stk)
                    tt = p.done("TE", TE.matmul(psC[0:1, 0:TS], onescol_b[:, 0:1], x2b[:, :],
                                                start=(cc == 0), stop=(cc == 7)))
                p.wait(V, "TE", tt)
                p.vbar(V.tensor_scalar(strow[0:1, 0:TS], psC[0:1, 0:TS], 1.0 / D, 1e-6,
                                       OP.mult, OP.add))
                vt = p.tick["V"]
                p.wait(S, "V", vt)
                p.sbar(S.activation(strow[0:1, 0:TS], strow[0:1, 0:TS], AF.Sqrt))
                p.wait(V, "S")
                p.vbar(V.reciprocal(strow[0:1, 0:TS], strow[0:1, 0:TS]))
                vt = p.done("V", V.tensor_copy(stat_b[0:1, 0:TS], strow[0:1, 0:TS]))
                p.wait(TE, "V", vt)
                tt = p.done("TE", TE.matmul(psC[:, 0:TS], onesrow_b[0:1, :],
                                            stat_b[0:1, 0:TS], start=True, stop=True))
                p.wait(V, "TE", tt)
                for cc in range(8):
                    p.vbar(V.tensor_tensor(mvfm[:, (8 + cc) * TS:(9 + cc) * TS],
                                           vecs[:, cc * TS:(cc + 1) * TS], psC[:, 0:TS],
                                           OP.mult))
                    p.done("V", V.tensor_scalar(mvfm[:, (8 + cc) * TS:(9 + cc) * TS],
                                                mvfm[:, (8 + cc) * TS:(9 + cc) * TS],
                                                rmsv_col[:, cc:cc + 1], None, OP.mult))
                for m in range(8):
                    for h in range(2):
                        b = 2 * m + h
                        if b == 0:
                            sw2_pre(dr["W_outb_s"], 0)
                        w = sw2_consume()
                        if b + 1 < 16:
                            sw2_pre(dr["W_outb_s"], b + 1)
                        for k8 in range(8):
                            k = h * 8 + k8
                            tt = p.done("TE", TE.matmul(psB[:, 0:TS],
                                                        w[:, k8 * 128:(k8 + 1) * 128],
                                                        mvfm[:, k * TS:(k + 1) * TS],
                                                        start=(k == 0), stop=(k == 15)))
                    p.wait(S, "TE", tt)
                    stk = p.done("S", S.activation(gatef[:, :], ogb[:, m * TS:(m + 1) * TS],
                                                   AF.Sigmoid))
                    p.wait(V, "S", stk)
                    p.wait(V, "TE", tt)
                    p.vbar(V.tensor_scalar(dump2[:, :], psB[:, 0:TS], boutb_col[:, m:m + 1],
                                           None, OP.add))
                    p.vbar(V.tensor_tensor(dump2[:, :], dump2[:, :], gatef[:, :], OP.mult))
                    vt = p.done("V", V.tensor_tensor(stream_c(m, sl), stream_c(m, sl),
                                                     dump2[:, :], OP.add))
                    p.wait(TE, "V", vt)

        if n_phases >= 4:
            p.arena_reset()
            e0b2 = p.aB(TS)
            e1b2 = p.aB(TS)
            d2b = p.aF(TS)
            d3b = p.aF(TS)
            ffn_phase(e0b2, e1b2, d2b, d3b, rowbuf, crow, browbuf01, colbuf)

        out_tm = x_tm_f
        for i in range(T // 128):
            p.wait(V, "Qd")
            p.wait(TE, "V")
            for c in range(8):
                p.wait(TE, "V")
                tt = p.done("TE", TE.transpose(psE[:, 0:128],
                                               stream_c(c)[:, i * 128:(i + 1) * 128], ident[:, :]))
                p.wait(V, "TE", tt)
                p.done("V", V.tensor_copy(out_tm[:, c * 128:(c + 1) * 128],
                                          psE[:, 0:128]))
            p.wait(Q, "V")
            td = p.done("Qd", Q.dma_start(out_h[i * 128:(i + 1) * 128, :], out_tm[:, :]))
            Q.wait_ge(p.sem["Qd"], td)
        Q.wait_ge(p.sem["Qd"], p.tick["Qd"])

    return nc


def kernel(**inputs):
    pre = _host_prepare(inputs)
    nc = build_program(DEBUG_PHASES)
    x = np.asarray(inputs["L0_attn_a_x"], np.float32)
    shared = {k: np.asarray(v) for k, v in pre.items()}
    in_maps = []
    for b in range(8):
        m = dict(shared)
        m["x_in"] = np.ascontiguousarray(x[b])
        in_maps.append(m)
    res = run_bass_kernel_spmd(nc, in_maps, core_ids=list(range(8)))
    return np.stack([np.asarray(r["out"], np.float32) for r in res.results], axis=0)
